# version 01 (final)
# baseline (speedup 1.0000x reference)
import contextlib
import numpy as np
import ml_dtypes
import concourse.bass as bass
import concourse.mybir as mybir
from concourse.bass_utils import run_bass_kernel_spmd

F32 = mybir.dt.float32
BF16 = mybir.dt.bfloat16
AF = mybir.ActivationFunctionType
ALU = mybir.AluOpType
AX = mybir.AxisListType

NCORES = 8
D = 4096
T = 1024
NF = D // 128
HFF = 11008
NHC = HFF // 128
DFOLD = 1024
HFOLD = 2752
HFOLDP = 2816
NSEG = 22
EPS = 1e-5
DEBUG_FFN0 = False

ENG2BLOCK = {"t": "tensor", "v": "vector", "s": "scalar", "g": "gpsimd", "y": "sync"}


class Rec:
    __slots__ = ("fname", "args", "kw", "incs")

    def __init__(self, fname, args, kw):
        self.fname = fname
        self.args = args
        self.kw = kw
        self.incs = []


class Ev:
    __slots__ = ("sem", "target")

    def __init__(self, sem, target):
        self.sem = sem
        self.target = target


class Eng:

    def __init__(self, pb, name):
        self._pb = pb
        self._n = name

    def __getattr__(self, fname):
        def emit(*args, **kw):
            pbx = self._pb
            e = self._n
            if e in "vsg":
                sem = "P_" + e
                prev = pbx.counts[sem]
                if prev > 0:
                    pbx.recs[e].append(Rec("wait_ge", (pbx.sems[sem], prev), {}))
                rec = Rec(fname, args, kw)
                pbx.recs[e].append(rec)
                pbx.last_real[e] = rec
                pbx.counts[sem] = prev + 1
                rec.incs.append((sem, 1))
                pbx.last_ev[e] = Ev(sem, prev + 1)
            else:
                rec = Rec(fname, args, kw)
                pbx.recs[e].append(rec)
                pbx.last_real[e] = rec
                pbx.last_ev[e] = None
            return rec
        return emit


class PB:
    def __init__(self, nc, es):
        self.nc = nc
        self.es = es
        self.recs = {e: [] for e in ENG2BLOCK}
        self.last_real = {e: None for e in ENG2BLOCK}
        self.last_ev = {e: None for e in ENG2BLOCK}
        self.sems = {}
        self.counts = {}
        for e in ENG2BLOCK:
            name = "P_" + e
            self.sems[name] = es.enter_context(nc.semaphore(name))
            self.counts[name] = 0
        self.t = Eng(self, "t")
        self.v = Eng(self, "v")
        self.s = Eng(self, "s")
        self.g = Eng(self, "g")
        self.y = Eng(self, "y")

    def _ksem(self, key):
        name = "K_" + key
        if name not in self.sems:
            self.sems[name] = self.es.enter_context(self.nc.semaphore(name))
            self.counts[name] = 0
        return name

    def tag(self, e):
        if self.last_ev[e] is not None:
            return self.last_ev[e]
        rec = self.last_real[e]
        assert rec is not None, f"tag({e}) with no op"
        sem = "P_" + e
        self.counts[sem] += 1
        rec.incs.append((sem, 1))
        ev = Ev(sem, self.counts[sem])
        self.last_ev[e] = ev
        return ev

    def wait(self, e, ev):
        if ev is None:
            return
        if ev.sem == "P_" + e:
            return
        self.recs[e].append(Rec("wait_ge", (self.sems[ev.sem], ev.target), {}))

    def waitf(self, e, ev):
        if ev is None:
            return
        self.recs[e].append(Rec("wait_ge", (self.sems[ev.sem], ev.target), {}))

    def dma(self, e, out, in_, key=None, **kw):
        if key is None:
            key = out.name
        sem = self._ksem(key)
        rec = Rec("dma_start", (), dict(out=out, in_=in_, **kw))
        self.recs[e].append(rec)
        self.last_real[e] = rec
        self.counts[sem] += 16
        rec.incs.append((sem, 16))
        if not hasattr(self, "dirty"):
            self.dirty = set()
        self.dirty.add(sem)
        ev = Ev(sem, self.counts[sem])
        self.last_ev[e] = ev
        return ev

    def dmat(self, e, out, in_, key=None):
        if key is None:
            key = out.name
        sem = self._ksem(key)
        rec = Rec("dma_start_transpose", (out, in_), {})
        self.recs[e].append(rec)
        self.last_real[e] = rec
        self.counts[sem] += 16
        rec.incs.append((sem, 16))
        if not hasattr(self, "dirty"):
            self.dirty = set()
        self.dirty.add(sem)
        ev = Ev(sem, self.counts[sem])
        self.last_ev[e] = ev
        return ev

    def collective(self, kind, op, groups, ins, outs, key="cc"):
        sem = self._ksem(key)
        rec = Rec("collective_compute", (kind, op),
                  dict(replica_groups=groups, ins=ins, outs=outs))
        self.recs["g"].append(rec)
        self.last_real["g"] = rec
        self.counts[sem] += 1
        rec.incs.append((sem, 1))
        if not hasattr(self, "dirty"):
            self.dirty = set()
        self.dirty.add(sem)
        ev = Ev(sem, self.counts[sem])
        self.last_ev["g"] = ev
        return ev

    def barrier(self, engines="tvsgy"):
        evs = []
        for e in engines:
            if self.last_real[e] is not None:
                evs.append(self.tag(e))
        for name in sorted(getattr(self, "dirty", set())):
            evs.append(Ev(name, self.counts[name]))
        for e in engines:
            for ev in evs:
                self.wait(e, ev)
        if len(engines) == 5:
            self.dirty = set()

    def finalize(self, block):
        for e, bname in ENG2BLOCK.items():
            recs = self.recs[e]

            def body(eng, recs=recs):
                for r in recs:
                    if r.fname == "wait_ge":
                        eng.wait_ge(r.args[0], r.args[1])
                        continue
                    inst = getattr(eng, r.fname)(*r.args, **r.kw)
                    for (sem, k) in r.incs:
                        inst.then_inc(self.sems[sem], k)

            getattr(block, bname)(body)


class Ctx:
    pass


def sb(nc, es, name, shape, dtype):
    return es.enter_context(nc.sbuf_tensor(name, shape, dtype))


def ps(nc, es, name, shape, dtype=F32):
    return es.enter_context(nc.psum_tensor(name, shape, dtype))



def _fold_segments(m):
    segs = []
    r0 = m * 128
    done = 0
    while done < 128:
        g = (r0 + done) % HFOLD
        p = (r0 + done) // HFOLD
        row = g // 128
        off = g % 128
        ln = min(128 - done, 128 - off, HFOLD - g)
        segs.append((done, row, off, ln, p == 0))
        done += ln
    return segs


def emit_ffn0(pb, c, W, pk, scale):
    ev_mm = [None]
    for ci in range(NF):
        for jh in range(2):
            buf = 0
            sl = slice(jh * 512, (jh + 1) * 512)
            xb = c.xb_st[buf]
            x2 = c.x2_st[buf]
            pb.wait("s", ev_mm[buf])
            pb.s.activation(x2[:], c.x[:, ci, sl], AF.Square)
            ev_sq = pb.tag("s")
            pb.wait("v", ev_mm[buf])
            pb.v.tensor_copy(xb[:], c.x[:, ci, sl])
            ev_xb = pb.tag("v")
            pb.wait("t", ev_sq)
            pb.wait("t", ev_xb)
            first = ci == 0
            last = ci == NF - 1
            pb.t.matmul(pk[0 + jh][0:1, :], c.ones_col[:], xb[:], start=first, stop=last)
            pb.t.matmul(pk[2 + jh][0:1, :], c.ones_col[:], x2[:], start=first, stop=last)
            pb.t.matmul(pk[4 + jh][0:2, :], c.a1t[:, ci, :], xb[:], start=first, stop=last)
            pb.t.matmul(pk[6 + jh][0:2, :], c.a3t[:, ci, :], xb[:], start=first, stop=last)
            ev_mm[buf] = pb.tag("t")
    ev_stats = pb.tag("t")

    r0, r1 = c.row0, c.row1
    pb.wait("v", ev_stats)
    for jh in range(2):
        sl = slice(jh * 512, (jh + 1) * 512)
        pb.v.tensor_scalar(r0[0:1, sl], pk[0 + jh][0:1, :], 1.0 / D, None, ALU.mult)
        pb.v.tensor_scalar(r1[0:1, sl], pk[2 + jh][0:1, :], 1.0 / D, None, ALU.mult)
    pb.v.tensor_tensor(c.f1[0:1, 0:T // 2], r0[0:1, 0:512], r0[0:1, 0:512], ALU.mult)
    pb.v.tensor_sub(r1[0:1, 0:512], r1[0:1, 0:512], c.f1[0:1, 0:T // 2])
    pb.v.tensor_tensor(c.f1[0:1, 0:T // 2], r0[0:1, 512:1024], r0[0:1, 512:1024], ALU.mult)
    pb.v.tensor_sub(r1[0:1, 512:1024], r1[0:1, 512:1024], c.f1[0:1, 0:T // 2])
    ev_var = pb.tag("v")
    pb.wait("s", ev_var)
    pb.s.activation(r1[0:1, :], r1[0:1, :], AF.Ln, bias=c.epsc[0:1, :])
    pb.s.activation(r1[0:1, :], r1[0:1, :], AF.Exp, scale=-0.5)
    ev_srt = pb.tag("s")
    pb.wait("v", ev_srt)
    pb.v.tensor_tensor(r0[0:1, :], r0[0:1, :], r1[0:1, :], ALU.mult)
    pb.v.tensor_scalar(r0[0:1, :], r0[0:1, :], 4.0, None, ALU.mult)
    ev_rows = pb.tag("v")

    pb.wait("t", ev_rows)
    for jh in range(2):
        sl = slice(jh * 512, (jh + 1) * 512)
        pb.t.matmul(pk[0 + jh][:, :], c.ones_row_f[:], r1[0:1, sl], start=True, stop=True)
        pb.t.matmul(pk[2 + jh][:, :], c.ones_row_f[:], r0[0:1, sl], start=True, stop=True)
    ev_rep = pb.tag("t")
    pb.wait("v", ev_rep)
    pb.v.tensor_copy(c.rstd_rep[:, 0:512], pk[0][:, :])
    pb.v.tensor_copy(c.rstd_rep[:, 512:1024], pk[1][:, :])
    pb.v.tensor_copy(c.m4rstd_rep[:, 0:512], pk[2][:, :])
    pb.v.tensor_copy(c.m4rstd_rep[:, 512:1024], pk[3][:, :])
    ev_reps = pb.tag("v")

    pb.wait("t", ev_reps)
    for ci in range(NF):
        pb.t.matmul(pk[0][0:2, 0:1], c.a1t[:, ci, :], c.ones_col[:], start=(ci == 0), stop=(ci == NF - 1))
        pb.t.matmul(pk[1][0:2, 0:1], c.a3t[:, ci, :], c.ones_col[:], start=(ci == 0), stop=(ci == NF - 1))
    ev_cs = pb.tag("t")
    pb.wait("v", ev_cs)
    pb.v.tensor_copy(c.csa[0:2, 0:1], pk[0][0:2, 0:1])
    pb.v.tensor_copy(c.csa[0:2, 1:2], pk[1][0:2, 0:1])
    pb.v.tensor_scalar(c.csa[0:2, :], c.csa[0:2, :], -0.25, None, ALU.mult)
    ev_csn = pb.tag("v")

    for (blo, tout, col) in ((4, c.t1b, 0), (6, c.t3b, 1)):
        for jh in range(2):
            sl = slice(jh * 512, (jh + 1) * 512)
            pb.v.tensor_tensor(c.f1[0:2, :], pk[blo + jh][0:2, :], c.rstd_rep[0:2, sl], ALU.mult)
            pb.v.scalar_tensor_tensor(c.f1[0:2, :], c.m4rstd_rep[0:2, sl],
                                      c.csa[0:2, col:col + 1], c.f1[0:2, :],
                                      ALU.mult, ALU.add)
            pb.v.tensor_scalar(tout[0:2, sl], c.f1[0:2, :], 0.5, None, ALU.mult)
    ev_t13 = pb.tag("v")

    for hb in range(2):
        tsl = slice(hb * 512, (hb + 1) * 512)

        for j in range(8):
            if hb == 0 and j == 0:
                pb.wait("g", ev_t13)
            pb.g.tensor_add(c.f1[:], c.x[:, j, tsl], c.x[:, j + 8, tsl])
            pb.g.tensor_add(c.f1[:], c.f1[:], c.x[:, j + 16, tsl])
            pb.g.tensor_add(c.f1[:], c.f1[:], c.x[:, j + 24, tsl])
            evg = pb.tag("g")
            pb.wait("v", evg)
            pb.v.tensor_tensor(c.f1[:], c.f1[:], c.rstd_rep[:, tsl], ALU.mult)
            pb.v.tensor_sub(c.xfold[:, j, :], c.f1[:], c.m4rstd_rep[:, tsl])
            evv = pb.tag("v")
            pb.wait("g", evv)
        ev_fold = pb.tag("v")

        ev_slab = {}
        ev_bs = {}
        ev_slab_free = {}
        for m in (0, 1):
            ev_slab[m] = _ffn0_slab_load(pb, c, W, m, m % 2)
            ev_bs[m] = _ffn0_bslab_load(pb, c, W, m, m % 2)
        ev_silu_done = {0: None, 1: None}
        ev_hm = [None] * NHC
        ev_t2mm = [None] * NHC
        for m in range(NHC):
            bi = m % 2
            gbank = pk[0] if bi == 0 else pk[2]
            pb.wait("t", ev_slab[m])
            pb.wait("t", ev_bs[m])
            if m == 0:
                pb.wait("t", ev_fold)
                pb.wait("t", ev_t13)
                pb.wait("t", ev_csn)
            pb.wait("t", ev_silu_done[bi])
            s1 = c.slab1[bi]
            s3 = c.slab3[bi]
            for k in range(8):
                pb.t.matmul(gbank[:, :], s1[:, k, :], c.xfold[:, k, :], start=(k == 0), stop=False)
            pb.t.matmul(gbank[:, :], c.bslab1[bi][:, 0:128], c.t1b[0:2, tsl], start=False, stop=True)
            ev_gate = pb.tag("t")
            if m >= 1:
                pb.wait("t", ev_hm[m - 1])
            for k in range(8):
                pb.t.matmul(pk[4][:, :], s3[:, k, :], c.xfold[:, k, :], start=(k == 0), stop=False)
            pb.t.matmul(pk[4][:, :], c.bslab3[bi][:, 0:128], c.t3b[0:2, tsl], start=False, stop=True)
            ev_up = pb.tag("t")
            ev_slab_free[m] = ev_up
            if m + 2 < NHC:
                pb.wait("g", ev_up)
                ev_slab[m + 2] = _ffn0_slab_load(pb, c, W, m + 2, bi)
                ev_bs[m + 2] = _ffn0_bslab_load(pb, c, W, m + 2, bi)
            pb.wait("s", ev_gate)
            if m >= 1:
                pb.wait("s", ev_hm[m - 1])
            pb.s.activation(c.sg_st[:], gbank[:, :], AF.Silu)
            ev_silu = pb.tag("s")
            ev_silu_done[bi] = ev_silu
            pb.wait("v", ev_silu)
            pb.wait("v", ev_up)
            if m >= 2:
                pb.wait("v", ev_t2mm[m - 2])
            hst = c.h_st[bi]
            pb.v.tensor_tensor(hst[:], c.sg_st[:], pk[4][:, :], ALU.mult)
            ev_h = pb.tag("v")
            ev_hm[m] = ev_h
            segs = _fold_segments(m)
            need_rot = any(so != off for (so, row, off, ln, fp) in segs)
            ev_rot = None
            if need_rot:
                pb.wait("t", ev_h)
                pb.t.matmul(pk[5][:, :], c.p64[:], hst[:], start=True, stop=True)
                ev_rot = pb.tag("t")
            for (so, row, off, ln, fp) in segs:
                if so == off:
                    if fp:
                        pb.v.tensor_copy(c.hfold[off:off + ln, row, :], hst[so:so + ln, :])
                    else:
                        pb.v.tensor_add(c.hfold[off:off + ln, row, :],
                                        c.hfold[off:off + ln, row, :], hst[so:so + ln, :])
                else:
                    assert (so - off) % 128 == 64
                    pb.wait("v", ev_rot)
                    if fp:
                        pb.v.tensor_copy(c.hfold[off:off + ln, row, :], pk[5][off:off + ln, :])
                    else:
                        pb.v.tensor_add(c.hfold[off:off + ln, row, :],
                                        c.hfold[off:off + ln, row, :], pk[5][off:off + ln, :])
            ev_fold_h = pb.tag("v")
            pb.wait("t", ev_h)
            pb.t.matmul(pk[6][0:2, :], c.a2t[:, m, :], hst[:, :], start=(m == 0), stop=(m == NHC - 1))
            ev_t2mm[m] = pb.tag("t")
        ev_gateloop = pb.tag("t")
        ev_lastfold = pb.tag("v")

        pb.wait("g", ev_lastfold)
        pb.g.memset(c.hfold[64:128, NSEG - 1, :], 0.0)
        ev_pad = pb.tag("g")

        pb.wait("v", ev_gateloop)
        pb.v.tensor_scalar(c.t2b[0:2, tsl], pk[6][0:2, :], 0.5, None, ALU.mult)
        ev_t2 = pb.tag("v")

        ev_ws = {(0, 0): _ffn0_w2_load(pb, c, W, 0, 0), (0, 1): _ffn0_w2_load(pb, c, W, 0, 1)}
        ev_b2 = {0: pb.dma("g", out=c.b2slab[0][:, 0:128], in_=W["b2T"][:, 0:128])}
        ev_half_mm = {0: None, 1: None}
        ev_out_done = {0: None, 1: None}
        for fo in range(NF):
            bi = fo % 2
            obank = pk[1] if bi == 0 else pk[3]
            if fo == 0:
                pb.wait("t", ev_pad)
                pb.wait("t", ev_t2)
            pb.wait("t", ev_out_done[bi])
            HS = NSEG // 2
            for jh2 in range(2):
                pb.wait("t", ev_ws[(fo, jh2)])
                if jh2 == 1:
                    pb.wait("t", ev_b2[fo])
                s2 = c.slab2[jh2]
                for j in range(HS):
                    pb.t.matmul(obank[:, :], s2[:, j, :], c.hfold[:, jh2 * HS + j, :],
                                start=(jh2 == 0 and j == 0), stop=False)
                if jh2 == 1:
                    pb.t.matmul(obank[:, :], c.b2slab[0][:, 0:128], c.t2b[0:2, tsl],
                                start=False, stop=True)
                ev_half_mm[jh2] = pb.tag("t")
                if fo + 1 < NF:
                    pb.wait("g", ev_half_mm[jh2])
                    ev_ws[(fo + 1, jh2)] = _ffn0_w2_load(pb, c, W, fo + 1, jh2)
                    if jh2 == 1:
                        ev_b2[fo + 1] = pb.dma("g", out=c.b2slab[0][:, 0:128],
                                               in_=W["b2T"][:, (fo + 1) * 128:(fo + 2) * 128])
            ev_o = ev_half_mm[1]
            pb.wait("v", ev_o)
            pb.v.scalar_tensor_tensor(c.x[:, fo, tsl], obank[:, :], float(scale),
                                      c.x[:, fo, tsl], ALU.mult, ALU.add)
            ev_out_done[bi] = pb.tag("v")
        pb.barrier(engines="tvg")


def _ffn0_slab_load(pb, c, W, m, buf):
    pb.dma("g", out=c.slab1[buf][:],
           in_=W["w1qT"].rearrange("(kc kp) h -> kp kc h", kp=128)[:, :, m * 128:(m + 1) * 128])
    return pb.dma("g", out=c.slab3[buf][:],
                  in_=W["w3qT"].rearrange("(kc kp) h -> kp kc h", kp=128)[:, :, m * 128:(m + 1) * 128])


def _ffn0_bslab_load(pb, c, W, m, buf):
    pb.dma("g", out=c.bslab1[buf][:, 0:128], in_=W["b1T"][:, m * 128:(m + 1) * 128])
    return pb.dma("g", out=c.bslab3[buf][:, 0:128], in_=W["b3T"][:, m * 128:(m + 1) * 128])


def _ffn0_w2_load(pb, c, W, fo, jh2):
    s0 = jh2 * (NSEG // 2)
    return pb.dma("g", out=c.slab2[jh2][:],
                  in_=W["w2qTp"].rearrange("(sc sp) o -> sp sc o", sp=128)[:, s0:s0 + NSEG // 2, fo * 128:(fo + 1) * 128])


def alloc_ffn0_bufs(nc, es, c):
    c.xfold = sb(nc, es, "xfold", [128, 8, 512], BF16)
    c.hfold = sb(nc, es, "hfold", [128, NSEG, 512], BF16)
    c.slab1 = [sb(nc, es, f"slab1_{i}", [128, 8, 128], BF16) for i in range(2)]
    c.slab3 = [sb(nc, es, f"slab3_{i}", [128, 8, 128], BF16) for i in range(2)]
    c.slab2 = [sb(nc, es, f"slab2_{i}", [128, NSEG // 2, 128], BF16) for i in range(2)]
    c.xb_st = [sb(nc, es, "xb_st0", [128, 512], BF16)]
    c.x2_st = [sb(nc, es, "x2_st0", [128, 512], BF16)]
    c.sg_st = sb(nc, es, "sg_st", [128, 512], BF16)
    c.h_st = [sb(nc, es, f"h_st{i}", [128, 512], BF16) for i in range(2)]
    c.f1 = sb(nc, es, "f1", [128, 512], F32)
    c.rstd_rep = sb(nc, es, "rstd_rep", [128, T], F32)
    c.m4rstd_rep = sb(nc, es, "m4rstd_rep", [128, T], BF16)
    c.a1t = sb(nc, es, "a1t", [128, NF, 2], BF16)
    c.a3t = sb(nc, es, "a3t", [128, NF, 2], BF16)
    c.a2t = sb(nc, es, "a2t", [128, NHC, 2], BF16)
    c.bslab1 = [sb(nc, es, f"bslab1_{i}", [2, 128], BF16) for i in range(2)]
    c.bslab3 = [sb(nc, es, f"bslab3_{i}", [2, 128], BF16) for i in range(2)]
    c.b2slab = [sb(nc, es, "b2slab_0", [2, 128], BF16)]
    c.t1b = sb(nc, es, "t1b", [2, T], BF16)
    c.t3b = sb(nc, es, "t3b", [2, T], BF16)
    c.t2b = sb(nc, es, "t2b", [2, T], BF16)
    c.row0 = sb(nc, es, "row0", [1, T], F32)
    c.row1 = sb(nc, es, "row1", [1, T], F32)
    c.csa = sb(nc, es, "csa", [2, 2], F32)


def emit_ffn0_const_loads(pb, c, W):
    pb.dma("g", out=c.a1t[:], in_=W["a1T"].rearrange("(c p) e -> p c e", p=128))
    pb.dma("g", out=c.a3t[:], in_=W["a3T"].rearrange("(c p) e -> p c e", p=128))
    pb.dma("g", out=c.a2t[:], in_=W["a2T"].rearrange("(c p) e -> p c e", p=128))


def emit_load_x(pb, c, x_in):
    for q in range(4):
        pb.dma("y", out=c.x[:, q * 8:(q + 1) * 8, :],
               in_=x_in.rearrange("(c p) t -> p c t", p=128)[:, q * 8:(q + 1) * 8, :])


def emit_store_x(pb, c, x_out):
    for q in range(4):
        pb.dma("y", out=x_out.rearrange("(c p) t -> p c t", p=128)[:, q * 8:(q + 1) * 8, :],
               in_=c.x[:, q * 8:(q + 1) * 8, :])
    pb.barrier(engines="y")


def build_ffn0_launch():
    nc = bass.Bass("TRN2", target_bir_lowering=False)
    es = contextlib.ExitStack()
    c = Ctx()
    with es:
        x_in = nc.dram_tensor("x_in", [D, T], F32, kind="ExternalInput").ap()
        x_out = nc.dram_tensor("x_out", [D, T], F32, kind="ExternalOutput").ap()
        W = {}
        W["w1qT"] = nc.dram_tensor("w1qT", [DFOLD, HFF], F32, kind="ExternalInput").ap()
        W["w3qT"] = nc.dram_tensor("w3qT", [DFOLD, HFF], F32, kind="ExternalInput").ap()
        W["w2qTp"] = nc.dram_tensor("w2qTp", [HFOLDP, D], F32, kind="ExternalInput").ap()
        W["a1T"] = nc.dram_tensor("a1T", [D, 2], F32, kind="ExternalInput").ap()
        W["a3T"] = nc.dram_tensor("a3T", [D, 2], F32, kind="ExternalInput").ap()
        W["a2T"] = nc.dram_tensor("a2T", [HFF, 2], F32, kind="ExternalInput").ap()
        W["b1T"] = nc.dram_tensor("b1T", [2, HFF], F32, kind="ExternalInput").ap()
        W["b3T"] = nc.dram_tensor("b3T", [2, HFF], F32, kind="ExternalInput").ap()
        W["b2T"] = nc.dram_tensor("b2T", [2, D], F32, kind="ExternalInput").ap()
        ones_col_in = nc.dram_tensor("ones_col", [128, 1], BF16, kind="ExternalInput").ap()
        ones_row_in = nc.dram_tensor("ones_row", [1, 128], F32, kind="ExternalInput").ap()
        epsc_in = nc.dram_tensor("epsc", [128, 1], F32, kind="ExternalInput").ap()
        p64_in = nc.dram_tensor("p64", [128, 128], BF16, kind="ExternalInput").ap()

        pb = PB(nc, es)
        c.x = sb(nc, es, "x", [128, NF, T], F32)
        alloc_ffn0_bufs(nc, es, c)
        c.ones_col = sb(nc, es, "onescol", [128, 1], BF16)
        c.ones_row_f = sb(nc, es, "onesrowf", [1, 128], F32)
        c.epsc = sb(nc, es, "epsc_sb", [128, 1], F32)
        c.p64 = sb(nc, es, "p64_sb", [128, 128], BF16)
        pk = [ps(nc, es, f"pb{i}", [128, 512], F32) for i in range(8)]

        dbg = {}
        if DEBUG_FFN0:
            dbg["xfold"] = nc.dram_tensor("dbg_xfold", [128, 8, 512], BF16, kind="ExternalOutput").ap()
            dbg["hfold"] = nc.dram_tensor("dbg_hfold", [128, NSEG, 512], BF16, kind="ExternalOutput").ap()
            dbg["t1b"] = nc.dram_tensor("dbg_t1b", [2, T], BF16, kind="ExternalOutput").ap()
            dbg["t3b"] = nc.dram_tensor("dbg_t3b", [2, T], BF16, kind="ExternalOutput").ap()
            dbg["t2b"] = nc.dram_tensor("dbg_t2b", [2, T], BF16, kind="ExternalOutput").ap()
            dbg["rstd"] = nc.dram_tensor("dbg_rstd", [128, T], F32, kind="ExternalOutput").ap()
            dbg["m4"] = nc.dram_tensor("dbg_m4", [128, T], BF16, kind="ExternalOutput").ap()
        with nc.Block() as block:
            emit_load_x(pb, c, x_in)
            pb.dma("y", out=c.ones_col[:], in_=ones_col_in)
            pb.dma("y", out=c.ones_row_f[:], in_=ones_row_in)
            pb.dma("y", out=c.epsc[:], in_=epsc_in)
            pb.dma("y", out=c.p64[:], in_=p64_in)
            emit_ffn0_const_loads(pb, c, W)
            pb.barrier()
            emit_ffn0(pb, c, W, pk, scale=0.5)
            pb.barrier()
            emit_store_x(pb, c, x_out)
            if DEBUG_FFN0:
                pb.dma("y", out=dbg["xfold"], in_=c.xfold[:])
                pb.dma("y", out=dbg["hfold"], in_=c.hfold[:])
                pb.dma("y", out=dbg["t1b"], in_=c.t1b[:])
                pb.dma("y", out=dbg["t3b"], in_=c.t3b[:])
                pb.dma("y", out=dbg["t2b"], in_=c.t2b[:])
                pb.dma("y", out=dbg["rstd"], in_=c.rstd_rep[:])
                pb.dma("y", out=dbg["m4"], in_=c.m4rstd_rep[:])
                pb.barrier(engines="y")
            pb.finalize(block)
    return nc



_NC_CACHE = {}


def _cached(name, fn):
    if name not in _NC_CACHE:
        _NC_CACHE[name] = fn()
    return _NC_CACHE[name]


def _prep_ffn0_weights(inputs, layer):
    p = {}
    p["w1qT"] = np.ascontiguousarray(inputs[f"{layer}_ffn_w1q"].T)
    p["w3qT"] = np.ascontiguousarray(inputs[f"{layer}_ffn_w3q"].T)
    w2T = inputs[f"{layer}_ffn_w2q"].T
    w2Tp = np.zeros((HFOLDP, D), np.float32)
    w2Tp[:HFOLD] = w2T
    p["w2qTp"] = w2Tp
    p["a1T"] = np.ascontiguousarray(inputs[f"{layer}_ffn_a1"].T)
    p["a3T"] = np.ascontiguousarray(inputs[f"{layer}_ffn_a3"].T)
    p["a2T"] = np.ascontiguousarray(inputs[f"{layer}_ffn_a2"].T)
    p["b1T"] = np.ascontiguousarray(inputs[f"{layer}_ffn_b1"].T)
    p["b3T"] = np.ascontiguousarray(inputs[f"{layer}_ffn_b3"].T)
    p["b2T"] = np.ascontiguousarray(inputs[f"{layer}_ffn_b2"].T)
    p["ones_col"] = np.ones((128, 1), ml_dtypes.bfloat16)
    p["ones_row"] = np.ones((1, 128), np.float32)
    p["epsc"] = np.full((128, 1), 1e-5, np.float32)
    P64 = np.zeros((128, 128), ml_dtypes.bfloat16)
    for m_ in range(128):
        P64[(m_ + 64) % 128, m_] = 1
    p["p64"] = P64
    return p


def shard_x(x_full):
    shards = []
    for cid in range(NCORES):
        b, h = cid // 2, cid % 2
        shards.append(np.ascontiguousarray(x_full[b, h * T:(h + 1) * T, :].T))
    return shards


def unshard_x(shards):
    out = np.empty((4, 2048, D), np.float32)
    for cid in range(NCORES):
        b, h = cid // 2, cid % 2
        out[b, h * T:(h + 1) * T, :] = shards[cid].T
    return out


def run_ffn0(x_shards, inputs, layer="L0"):
    nc = _cached("ffn0", build_ffn0_launch)
    wp = _prep_ffn0_weights(inputs, layer)
    in_maps = [dict(wp, x_in=x_shards[cid]) for cid in range(NCORES)]
    res = run_bass_kernel_spmd(nc, in_maps, core_ids=list(range(NCORES)))
    return [res.results[cid]["x_out"] for cid in range(NCORES)]


def kernel(**inputs):
    x = shard_x(np.asarray(inputs["L0_attn_hidden_states"], np.float32))
    x = run_ffn0(x, inputs, "L0")
    x = run_mamba(x, inputs)
    x = run_ffn0(x, inputs, "L0")
    x = run_ffn1(x, inputs)
    x = run_attn(x, inputs)
    x = run_ffn1(x, inputs)
    return unshard_x(x)



DS = 4096
SST = 128
CONV_DIM = DS + 2 * SST
NCC = 34
PROJ = 8512
NPC = 67
Q = 128
NCH = 8
GROUPS = [[0, 1], [2, 3], [4, 5], [6, 7]]


class Arena:
    def __init__(self, nc, es, name, nbytes):
        self.nb = nbytes
        self.t = es.enter_context(nc.sbuf_tensor(name, [128, nbytes], mybir.dt.uint8))

    def carve(self, off, shape, dtype):
        esz = 4 if dtype == F32 else 2
        n = 1
        for s_ in shape[1:]:
            n *= s_
        assert off % 32 == 0 and off + n * esz <= self.nb, (off, shape)
        ap = self.t.ap()[0:shape[0], off:off + n * esz].bitcast(dtype)
        if len(shape) == 3:
            ap = ap.rearrange("p (a b) -> p a b", a=shape[1])
        elif len(shape) == 4:
            ap = ap.rearrange("p (a b c) -> p a b c", a=shape[1], b=shape[2])
        return ap


def bcast3(ap2d, inner):
    return ap2d.unsqueeze(2).broadcast_to((ap2d.shape[0], ap2d.shape[1], inner))


def bcast3o(ap2d, outer):
    return ap2d.unsqueeze(1).broadcast_to((ap2d.shape[0], outer, ap2d.shape[1]))


def build_mamba_launch():
    nc = bass.Bass("TRN2", target_bir_lowering=False)
    es = contextlib.ExitStack()
    c = Ctx()
    with es:
        x_in = nc.dram_tensor("x_in", [D, T], F32, kind="ExternalInput").ap()
        x_out = nc.dram_tensor("x_out", [D, T], F32, kind="ExternalOutput").ap()
        W = {}
        W["w_in"] = nc.dram_tensor("w_in", [D, NPC * 128], F32, kind="ExternalInput").ap()
        W["w_out"] = nc.dram_tensor("w_out", [DS, D], F32, kind="ExternalInput").ap()
        W["mup"] = nc.dram_tensor("mup", [128, NPC], F32, kind="ExternalInput").ap()
        W["lnw"] = nc.dram_tensor("lnw", [128, NF], F32, kind="ExternalInput").ap()
        W["cw"] = nc.dram_tensor("cw", [128, NCC, 4], F32, kind="ExternalInput").ap()
        W["A_log"] = nc.dram_tensor("A_log", [64, 1], F32, kind="ExternalInput").ap()
        W["dt_bias"] = nc.dram_tensor("dt_bias", [64, 1], F32, kind="ExternalInput").ap()
        W["Drow"] = nc.dram_tensor("Drow", [1, 64], F32, kind="ExternalInput").ap()
        W["nwrow"] = nc.dram_tensor("nwrow", [1, DS], F32, kind="ExternalInput").ap()
        W["ones_col"] = nc.dram_tensor("ones_col", [128, 1], BF16, kind="ExternalInput").ap()
        W["ones_row"] = nc.dram_tensor("ones_row", [1, 128], F32, kind="ExternalInput").ap()
        W["epsc"] = nc.dram_tensor("epsc", [128, 1], F32, kind="ExternalInput").ap()
        W["Utri"] = nc.dram_tensor("Utri", [128, 128], BF16, kind="ExternalInput").ap()
        W["ident"] = nc.dram_tensor("identb", [128, 128], BF16, kind="ExternalInput").ap()
        W["allone"] = nc.dram_tensor("allone", [128, 128], BF16, kind="ExternalInput").ap()
        W["lneg4"] = nc.dram_tensor("lneg4", [128, 512], BF16, kind="ExternalInput").ap()
        W["pmask"] = nc.dram_tensor("pmask", [128, 1], F32, kind="ExternalInput").ap()

        import os
        mk = "ExternalOutput" if os.environ.get("MAMBA_DBG") else "Internal"
        z_hbm = nc.dram_tensor("z_hbm", [DS, T], BF16, kind=mk).ap()
        xbc_hbm = nc.dram_tensor("xbc_hbm", [CONV_DIM, T], BF16, kind=mk).ap()
        xbcs_hbm = nc.dram_tensor("xbcs_hbm", [CONV_DIM, T], BF16, kind=mk).ap()
        hadd_hbm = nc.dram_tensor("hadd_hbm", [NCH * 128, DS], BF16, kind=mk).ap()
        yi_hbm = nc.dram_tensor("yi_hbm", [NCH * 128, DS], BF16, kind=mk).ap()
        y4_hbm = nc.dram_tensor("y4_hbm", [T, DS], BF16, kind=mk).ap()
        cc1_in = nc.dram_tensor("cc1_in", [128, 128], BF16, kind="Internal").ap()
        cc1_out = nc.dram_tensor("cc1_out", [256, 128], BF16, kind="Internal").ap()
        cc2_in = nc.dram_tensor("cc2_in", [128, DS], BF16, kind="Internal").ap()
        cc2_out = nc.dram_tensor("cc2_out", [256, DS], BF16, kind="Internal").ap()
        SCR = dict(z=z_hbm, xbc=xbc_hbm, xbcs=xbcs_hbm, hadd=hadd_hbm, yi=yi_hbm,
                   y4=y4_hbm, c1i=cc1_in, c1o=cc1_out, c2i=cc2_in, c2o=cc2_out)

        pb = PB(nc, es)
        ar = Arena(nc, es, "arena", 200704)
        pk = [ps(nc, es, f"pb{i}", [128, 512], F32) for i in range(8)]
        emit_mamba(pb, c, ar, W, SCR, pk, x_in, x_out, nc)
    return nc


def emit_mamba(pb, c, ar, W, SCR, pk, x_in, x_out, nc):
    c.lnw = ar.carve(0, [128, NF], F32)
    c.mup = ar.carve(128, [128, NPC], F32)
    c.cw = ar.carve(416, [128, NCC, 4], F32)
    c.Acol = ar.carve(960, [64, 1], F32)
    c.dtb = ar.carve(992, [64, 1], F32)
    c.onescol = ar.carve(1024, [128, 1], BF16)
    c.onesrow = ar.carve(1056, [1, 128], F32)
    c.epsc = ar.carve(1568, [128, 1], F32)
    c.U = ar.carve(1600, [128, 128], BF16)
    c.ident = ar.carve(1856, [128, 128], BF16)
    c.allone = ar.carve(2112, [128, 128], BF16)
    c.pmask = ar.carve(2368, [128, 1], F32)
    c.Drep = ar.carve(2400, [128, 64], BF16)
    c.lneg4 = ar.carve(2528, [128, 512], BF16)
    c.e0_all = ar.carve(3552, [128, NCH, 64], BF16)
    c.datot_all = ar.carve(4576, [128, NCH, 64], BF16)
    c.dt_fm = ar.carve(5600, [64, T], F32)
    c.dtA_fm = ar.carve(9696, [64, T], BF16)
    c.dtb_fm = ar.carve(11744, [64, T], BF16)
    c.nw_rep = ar.carve(13792, [128, DS], BF16)
    c.halo = ar.carve(21984, [128, NCC, 3], BF16)
    c.halopad = ar.carve(21984, [128, 128], BF16)
    c.h = ar.carve(22240, [128, DS], BF16)
    c.row0 = ar.carve(30464, [1, T], F32)
    c.row1 = ar.carve(34560, [1, T], F32)
    c.rowa = ar.carve(38656, [1, T], F32)
    c.cs_rep = ar.carve(42752, [128, T], F32)
    c.mcs_rep = ar.carve(46848, [128, T], F32)
    PH = 50944

    with nc.Block() as block:
        pb.dma("y", out=c.lnw[:], in_=W["lnw"])
        pb.dma("y", out=c.mup[:], in_=W["mup"])
        pb.dma("y", out=c.cw[:], in_=W["cw"])
        pb.dma("y", out=c.Acol[:], in_=W["A_log"])
        pb.dma("y", out=c.dtb[:], in_=W["dt_bias"])
        pb.dma("y", out=c.onescol[:], in_=W["ones_col"])
        pb.dma("y", out=c.onesrow[:], in_=W["ones_row"])
        pb.dma("y", out=c.epsc[:], in_=W["epsc"])
        pb.dma("y", out=c.U[:], in_=W["Utri"])
        pb.dma("y", out=c.ident[:], in_=W["ident"])
        pb.dma("y", out=c.allone[:], in_=W["allone"])
        pb.dma("y", out=c.lneg4[:], in_=W["lneg4"])
        pb.dma("y", out=c.pmask[:], in_=W["pmask"])
        pb.barrier()

        import os
        stages = int(os.environ.get("MAMBA_STAGES", "9"))
        _mamba_p0_ln(pb, c, ar, pk, x_in, PH)
        if stages >= 2:
            _mamba_p1_inproj(pb, c, ar, W, SCR, pk, PH)
        if stages >= 3:
            _mamba_e1(pb, c, SCR)
            _mamba_p2_dt(pb, c, ar, W, pk)
        pb.barrier()
        if stages >= 4:
            _mamba_p3_conv(pb, c, ar, SCR, PH)
            pb.barrier()
        if stages >= 5:
            _mamba_p4_scanA(pb, c, ar, W, SCR, pk, PH)
            pb.barrier()
        if stages >= 6:
            _mamba_chain_e2(pb, c, ar, SCR, pk, PH)
        if stages >= 7:
            _mamba_p5_scanB(pb, c, ar, W, SCR, pk, PH)
            pb.barrier()
        import os as _os
        if _os.environ.get("MAMBA_DBG"):
            dbg_e0 = nc.dram_tensor("dbg_e0", [128, NCH, 64], BF16, kind="ExternalOutput").ap()
            dbg_dat = nc.dram_tensor("dbg_dat", [128, NCH, 64], BF16, kind="ExternalOutput").ap()
            dbg_h = nc.dram_tensor("dbg_h", [128, DS], BF16, kind="ExternalOutput").ap()
            dbg_cs = nc.dram_tensor("dbg_cs", [128, T], F32, kind="ExternalOutput").ap()
            dbg_halo = nc.dram_tensor("dbg_halo", [128, NCC * 3], BF16, kind="ExternalOutput").ap()
            pb.dma("y", out=dbg_halo, in_=c.halopad[:, 0:NCC * 3])
            dbg_c1i = nc.dram_tensor("dbg_c1i", [128, 128], BF16, kind="ExternalOutput").ap()
            dbg_c1o = nc.dram_tensor("dbg_c1o", [256, 128], BF16, kind="ExternalOutput").ap()
            pb.dma("y", out=dbg_c1i, in_=SCR["c1i"])
            pb.dma("y", out=dbg_c1o.rearrange("(c p) n -> p c n", c=2), in_=SCR["c1o"].rearrange("(c p) n -> p c n", c=2))
            dbg_xln = nc.dram_tensor("dbg_xln", [128, NF, T], BF16, kind="ExternalOutput").ap()
            pb.dma("y", out=dbg_cs, in_=c.cs_rep[:])
            pb.dma("y", out=dbg_xln, in_=c.xln2[:])
            pb.dma("y", out=dbg_e0, in_=c.e0_all[:])
            pb.dma("y", out=dbg_dat, in_=c.datot_all[:])
            pb.dma("y", out=dbg_h, in_=c.h[:])
            pb.barrier(engines="y")
        if stages >= 8:
            _mamba_p6_outproj(pb, c, ar, W, SCR, pk, x_in, x_out)
        else:
            for q in range(4):
                pb.dma("y", out=x_out.rearrange("(c p) t -> p c t", p=128)[:, q * 8:(q + 1) * 8, :],
                       in_=x_in.rearrange("(c p) t -> p c t", p=128)[:, q * 8:(q + 1) * 8, :])
            pb.barrier(engines="y")
        pb.finalize(block)


def _mamba_p0_ln(pb, c, ar, pk, x_in, PH):
    xst = [ar.carve(PH + 68 * 1024, [128, T], F32), ar.carve(PH + 72 * 1024, [128, T], F32)]
    x2b = ar.carve(PH + 76 * 1024, [128, 512], BF16)
    xbb = ar.carve(PH + 77 * 1024, [128, 512], BF16)
    c.xln2 = ar.carve(PH, [128, NF, T], BF16)
    xv = x_in.rearrange("(c p) t -> p c t", p=128)
    ev_ld = {0: pb.dma("y", out=xst[0][:], in_=xv[:, 0, :], key="p0x0"),
             1: pb.dma("y", out=xst[1][:], in_=xv[:, 1, :], key="p0x1")}
    ev_mm = [None, None]
    ev_mm_last = None
    for ci in range(NF):
        bi = ci % 2
        for jh in range(2):
            sl = slice(jh * 512, (jh + 1) * 512)
            pb.wait("s", ev_ld[ci])
            pb.wait("s", ev_mm_last)
            pb.s.activation(x2b[:], xst[bi][:, sl], AF.Square)
            ev_sq = pb.tag("s")
            pb.wait("v", ev_ld[ci])
            pb.wait("v", ev_mm_last)
            pb.v.tensor_copy(xbb[:], xst[bi][:, sl])
            ev_xb = pb.tag("v")
            pb.wait("t", ev_sq)
            pb.wait("t", ev_xb)
            pb.t.matmul(pk[0 + jh][0:1, :], c.onescol[:], xbb[:], start=(ci == 0), stop=(ci == NF - 1))
            pb.t.matmul(pk[2 + jh][0:1, :], c.onescol[:], x2b[:], start=(ci == 0), stop=(ci == NF - 1))
            ev_mm[bi] = pb.tag("t")
            ev_mm_last = ev_mm[bi]
        if ci + 2 < NF:
            pb.wait("y", ev_mm[bi])
            ev_ld[ci + 2] = pb.dma("y", out=xst[bi][:], in_=xv[:, ci + 2, :], key=f"p0x{bi}")
    ev_stats = pb.tag("t")

    r0, r1, ra = c.row0, c.row1, c.rowa
    pb.wait("v", ev_stats)
    for jh in range(2):
        sl = slice(jh * 512, (jh + 1) * 512)
        pb.v.tensor_scalar(r0[0:1, sl], pk[0 + jh][0:1, :], 1.0 / D, None, ALU.mult)
        pb.v.tensor_scalar(r1[0:1, sl], pk[2 + jh][0:1, :], 1.0 / D, None, ALU.mult)
    pb.v.tensor_tensor(ra[0:1, :], r0[0:1, :], r0[0:1, :], ALU.mult)
    pb.v.tensor_sub(r1[0:1, :], r1[0:1, :], ra[0:1, :])
    ev_v = pb.tag("v")
    pb.wait("s", ev_v)
    pb.s.activation(ra[0:1, :], r1[0:1, :], AF.Ln, bias=c.epsc[0:1, :])
    ev_s = pb.tag("s")
    pb.wait("v", ev_s)
    pb.wait("s", ev_s)
    pb.s.activation(c.cs_rep[0:1, 0:T], ra[0:1, :], AF.Exp, scale=-0.5)
    pb.s.activation(ra[0:1, :], ra[0:1, :], AF.Exp, scale=-1.0)
    ev_s = pb.tag("s")
    pb.wait("v", ev_s)
    pb.v.tensor_tensor(r1[0:1, :], r1[0:1, :], ra[0:1, :], ALU.mult)
    pb.v.tensor_scalar(r1[0:1, :], r1[0:1, :], 1e-5, None, ALU.add)
    ev_v = pb.tag("v")
    pb.wait("s", ev_v)
    pb.s.activation(r1[0:1, :], r1[0:1, :], AF.Ln)
    pb.s.activation(r1[0:1, :], r1[0:1, :], AF.Exp, scale=-0.5)
    ev_s = pb.tag("s")
    pb.wait("v", ev_s)
    pb.v.tensor_tensor(r1[0:1, :], r1[0:1, :], c.cs_rep[0:1, 0:T], ALU.mult)
    pb.v.tensor_tensor(r0[0:1, :], r0[0:1, :], r1[0:1, :], ALU.mult)
    ev_rows = pb.tag("v")
    pb.wait("t", ev_rows)
    for jh in range(2):
        sl = slice(jh * 512, (jh + 1) * 512)
        pb.t.matmul(pk[0 + jh][:, :], c.onesrow[:], r1[0:1, sl], start=True, stop=True)
        pb.t.matmul(pk[2 + jh][:, :], c.onesrow[:], r0[0:1, sl], start=True, stop=True)
    ev_rep = pb.tag("t")
    pb.wait("v", ev_rep)
    pb.v.tensor_copy(c.cs_rep[:, 0:512], pk[0][:, :])
    pb.v.tensor_copy(c.cs_rep[:, 512:1024], pk[1][:, :])
    pb.v.tensor_copy(c.mcs_rep[:, 0:512], pk[2][:, :])
    pb.v.tensor_copy(c.mcs_rep[:, 512:1024], pk[3][:, :])
    ev_reps = pb.tag("v")
    pb.wait("y", ev_reps)
    ev_ld2 = {0: pb.dma("y", out=xst[0][:], in_=xv[:, 0, :], key="p0x0"),
              1: pb.dma("y", out=xst[1][:], in_=xv[:, 1, :], key="p0x1")}
    ev_ap = [None, None]
    for ci in range(NF):
        bi = ci % 2
        pb.wait("v", ev_ld2[ci])
        pb.v.tensor_tensor(xst[bi][:], xst[bi][:], c.cs_rep[:, :], ALU.mult)
        pb.v.tensor_sub(xst[bi][:], xst[bi][:], c.mcs_rep[:, :])
        pb.v.tensor_scalar(c.xln2[:, ci, :], xst[bi][:], c.lnw[:, ci:ci + 1], None, ALU.mult)
        ev_ap[bi] = pb.tag("v")
        if ci + 2 < NF:
            pb.wait("y", ev_ap[bi])
            ev_ld2[ci + 2] = pb.dma("y", out=xst[bi][:], in_=xv[:, ci + 2, :], key=f"p0x{bi}")
    pb.barrier(engines="tvsy")


def _mamba_p1_inproj(pb, c, ar, W, SCR, pk, PH):
    slab = [ar.carve(PH + 80 * 1024, [128, NF, 128], BF16),
            ar.carve(PH + 88 * 1024, [128, NF, 128], BF16)]
    est = [ar.carve(PH + 96 * 1024, [128, T], BF16), ar.carve(PH + 98 * 1024, [128, T], BF16)]
    wv = W["w_in"].rearrange("(kc kp) o -> kp kc o", kp=128)
    ev_slab = {0: pb.dma("g", out=slab[0][:], in_=wv[:, :, 0:128], key="p1s0"),
               1: pb.dma("g", out=slab[1][:], in_=wv[:, :, 128:256], key="p1s1")}
    ev_ev = {0: None, 1: None}
    for m in range(NPC):
        bi = m % 2
        pbank = (pk[0], pk[1]) if bi == 0 else (pk[2], pk[3])
        rows = 128 if m < NPC - 1 else 64
        pb.wait("t", ev_slab[m])
        pb.wait("t", ev_ev[bi])
        for k in range(NF):
            pb.t.matmul(pbank[0][0:rows, :], slab[bi][:, k, 0:rows], c.xln2[:, k, 0:512],
                        start=(k == 0), stop=(k == NF - 1))
            pb.t.matmul(pbank[1][0:rows, :], slab[bi][:, k, 0:rows], c.xln2[:, k, 512:1024],
                        start=(k == 0), stop=(k == NF - 1))
        ev_mm = pb.tag("t")
        if m + 2 < NPC:
            pb.wait("g", ev_mm)
            ev_slab[m + 2] = pb.dma("g", out=slab[bi][:], in_=wv[:, :, (m + 2) * 128:(m + 3) * 128], key=f"p1s{bi}")
        if m < 32:
            pb.wait("s", ev_ev[bi])
            pb.wait("s", ev_mm)
            pb.s.activation(est[bi][:, 0:512], pbank[0][:, :], AF.Silu, scale=c.mup[:, m:m + 1])
            pb.s.activation(est[bi][:, 512:1024], pbank[1][:, :], AF.Silu, scale=c.mup[:, m:m + 1])
            ev_act = pb.tag("s")
            pb.wait("y", ev_act)
            ev_ev[bi] = pb.dma("y", out=SCR["z"].rearrange("(c p) t -> p c t", p=128)[:, m, :], in_=est[bi][:], key=f"p1e{bi}")
        elif m < NPC - 1:
            mc = m - 32
            pb.wait("v", ev_ev[bi])
            pb.wait("v", ev_mm)
            pb.v.tensor_scalar(est[bi][:, 0:512], pbank[0][:, :], c.mup[:, m:m + 1], None, ALU.mult)
            pb.v.tensor_scalar(est[bi][:, 512:1024], pbank[1][:, :], c.mup[:, m:m + 1], None, ALU.mult)
            pb.v.tensor_scalar(c.halo[:, mc, :], pbank[1][:, 509:512], c.mup[:, m:m + 1], None, ALU.mult)
            ev_cp = pb.tag("v")
            pb.wait("y", ev_cp)
            ev_ev[bi] = pb.dma("y", out=SCR["xbc"].rearrange("(c p) t -> p c t", p=128)[:, mc, :], in_=est[bi][:], key=f"p1e{bi}")
        else:
            pb.wait("v", ev_ev[bi])
            pb.wait("v", ev_mm)
            pb.v.tensor_scalar(c.dt_fm[:, 0:512], pbank[0][0:64, :], c.mup[0:64, m:m + 1], None, ALU.mult)
            pb.v.tensor_scalar(c.dt_fm[:, 512:1024], pbank[1][0:64, :], c.mup[0:64, m:m + 1], None, ALU.mult)
            ev_ev[bi] = pb.tag("v")
    pb.barrier(engines="tvsgy")


def _mamba_e1(pb, c, SCR):
    ev = pb.dma("g", out=SCR["c1i"], in_=c.halopad[:])
    pb.wait("g", ev)
    ev_cc = pb.collective("AllGather", ALU.bypass, GROUPS, [SCR["c1i"]], [SCR["c1o"]], key="cc1")
    pb.wait("g", ev_cc)
    ev2 = pb.dma("g", out=c.halopad[:], in_=SCR["c1o"][0:128, :], key="halo")
    pb.wait("v", ev2)
    pb.v.tensor_scalar(c.halopad[:], c.halopad[:], c.pmask[:, 0:1], None, ALU.mult)
    pb.tag("v")


def _mamba_p2_dt(pb, c, ar, W, pk):
    pb.s.activation(c.Acol[:], c.Acol[:], AF.Exp)
    ev = pb.tag("s")
    pb.wait("v", ev)
    pb.v.tensor_scalar(c.Acol[:], c.Acol[:], -1.0, None, ALU.mult)
    pb.v.tensor_scalar(c.dt_fm[:], c.dt_fm[:], c.dtb[:, 0:1], None, ALU.add)
    ev_v = pb.tag("v")
    pb.wait("s", ev_v)
    for jh in range(2):
        sl = slice(jh * 512, (jh + 1) * 512)
        pb.s.activation(pk[jh][0:64, :], c.dt_fm[:, sl], AF.Abs)
        pb.s.activation(pk[jh][0:64, :], pk[jh][0:64, :], AF.Exp, scale=-1.0)
        pb.s.activation(pk[jh][0:64, :], pk[jh][0:64, :], AF.Ln, bias=1.0)
    ev_s = pb.tag("s")
    pb.wait("v", ev_s)
    pb.v.tensor_scalar_max(c.dt_fm[:], c.dt_fm[:], 0.0)
    pb.v.tensor_add(c.dt_fm[:, 0:512], c.dt_fm[:, 0:512], pk[0][0:64, :])
    pb.v.tensor_add(c.dt_fm[:, 512:1024], c.dt_fm[:, 512:1024], pk[1][0:64, :])
    pb.v.tensor_scalar(c.dtA_fm[:], c.dt_fm[:], c.Acol[:, 0:1], None, ALU.mult)
    pb.v.tensor_copy(c.dtb_fm[:], c.dt_fm[:])
    pb.tag("v")


def _mamba_p3_conv(pb, c, ar, SCR, PH):
    xt = ar.carve(PH, [128, NCC, 515], BF16)
    cv = [ar.carve(PH + 36 * 1024, [128, 512], BF16), ar.carve(PH + 37 * 1024, [128, 512], BF16)]
    sl_ = [ar.carve(PH + 38 * 1024, [128, 512], BF16), ar.carve(PH + 39 * 1024, [128, 512], BF16)]
    xbv = SCR["xbc"].rearrange("(c p) t -> p c t", p=128)
    xsv = SCR["xbcs"].rearrange("(c p) t -> p c t", p=128)
    for tt in range(2):
        if tt == 0:
            ev_ld = pb.dma("y", out=xt[:, :, 3:515], in_=xbv[:, :, 0:512], key="cvx")
            pb.wait("v", ev_ld)
            pb.v.tensor_copy(xt[:, :, 0:3], c.halo[:])
            ev_ld = pb.tag("v")
        else:
            ev_ld = pb.dma("y", out=xt[:, :, 0:515], in_=xbv[:, :, 509:1024], key="cvx")
        ev_store = {0: None, 1: None}
        for fc in range(NCC):
            bi = fc % 2
            eng = "v"
            E = getattr(pb, eng)
            pb.wait(eng, ev_ld)
            pb.wait(eng, ev_store[bi])
            E.tensor_scalar(cv[bi][:], xt[:, fc, 0:512], c.cw[:, fc, 0:1], None, ALU.mult)
            for kk in range(1, 4):
                E.scalar_tensor_tensor(cv[bi][:], xt[:, fc, kk:kk + 512], c.cw[:, fc, kk:kk + 1],
                                       cv[bi][:], ALU.mult, ALU.add)
            ev_c = pb.tag(eng)
            pb.wait("s", ev_c)
            pb.wait("s", ev_store[bi])
            pb.s.activation(sl_[bi][:], cv[bi][:], AF.Silu)
            ev_s = pb.tag("s")
            pb.wait("y", ev_s)
            ev_store[bi] = pb.dma("y", out=xsv[:, fc, tt * 512:(tt + 1) * 512], in_=sl_[bi][:], key=f"cvs{bi}")
        pb.barrier(engines="tvsgy")


def _mamba_p4_scanA(pb, c, ar, W, SCR, pk, PH):
    xs_tm = ar.carve(PH, [128, DS], BF16)
    xs2 = ar.carve(PH + 8 * 1024, [128, DS], BF16)
    Mq = ar.carve(PH + 16 * 1024, [128, 2048], BF16)
    rq = ar.carve(PH + 20 * 1024, [128, 2048], BF16)
    GM = ar.carve(PH + 24 * 1024, [128, 128], BF16)
    Btm = ar.carve(PH + 24 * 1024 + 256, [128, 128], BF16)
    Bfm = ar.carve(PH + 24 * 1024 + 512, [128, 128], BF16)
    Cfm = ar.carve(PH + 24 * 1024 + 768, [128, 128], BF16)
    dta_tm = ar.carve(PH + 25 * 1024, [128, 64], BF16)
    dtb_tm = ar.carve(PH + 30 * 1024, [128, 64], BF16)
    ldt_tm = ar.carve(PH + 25 * 1024 + 128, [128, 64], F32)
    negs = ar.carve(PH + 25 * 1024 + 384, [128, 64], F32)
    nsldt = ar.carve(PH + 25 * 1024 + 640, [128, 64], F32)
    w2b = ar.carve(PH + 25 * 1024 + 896, [128, 64], BF16)
    sd2 = ar.carve(PH + 26 * 1024, [128, 64], F32)
    ystg = ar.carve(PH + 27 * 1024, [128, 2048], BF16)
    sdq = ar.carve(PH + 31 * 1024, [128, 2048], F32)
    import os
    sub = int(os.environ.get("MAMBA_P4SUB", "9"))
    xbsv = SCR["xbcs"]
    xbsc = xbsv.rearrange("(c p) t -> p c t", p=128)
    for ch in range(NCH):
        t0 = ch * Q
        e_xs = pb.dmat("y", xs_tm[:], xbsv[0:DS, t0:t0 + Q], key="p4xs")
        e_btm = pb.dmat("y", Btm[:], xbsv[DS:DS + 128, t0:t0 + Q], key="p4btm")
        e_bfm = pb.dma("y", out=Bfm[:], in_=xbsc[:, 32, t0:t0 + Q], key="p4bfm")
        e_cfm = pb.dma("y", out=Cfm[:], in_=xbsc[:, 33, t0:t0 + Q], key="p4cfm")
        if sub < 2:
            pb.barrier(engines="tvsgy")
            continue
        ev_dta = pb.dmat("y", dta_tm[:], c.dtA_fm[:, t0:t0 + Q], key="p4dta")
        ev_trd = pb.dmat("y", dtb_tm[:], c.dtb_fm[:, t0:t0 + Q], key="p4dtb")
        pb.wait("s", ev_trd)
        pb.s.activation(ldt_tm[:], dtb_tm[:], AF.Ln)
        ev_ldt = pb.tag("s")
        pb.wait("t", ev_dta)
        pb.wait("v", ev_dta)
        pb.t.matmul(pk[4][:, 0:64], c.U[:], dta_tm[:], start=True, stop=True)
        ev_s = pb.tag("t")
        pb.t.matmul(pk[5][0:1, 0:64], c.onescol[:], dta_tm[:], start=True, stop=True)
        ev_tot = pb.tag("t")
        pb.wait("s", ev_s)
        pb.s.activation(c.e0_all[:, ch, :], pk[4][:, 0:64], AF.Exp)
        ev_e0 = pb.tag("s")
        pb.wait("v", ev_s)
        pb.v.tensor_scalar(negs[:], pk[4][:, 0:64], -1.0, None, ALU.mult)
        ev_negs = pb.tag("v")
        pb.wait("v", ev_tot)
        pb.v.tensor_copy(c.row0[0:1, 0:64], pk[5][0:1, 0:64])
        ev_totr = pb.tag("v")
        pb.wait("t", ev_totr)
        pb.wait("t", ev_e0)
        pb.wait("t", ev_negs)
        pb.t.matmul(pk[4][:, 0:64], c.onesrow[:], c.row0[0:1, 0:64], start=True, stop=True)
        ev_totrep = pb.tag("t")
        pb.wait("s", ev_totrep)
        pb.s.activation(c.datot_all[:, ch, :], pk[4][:, 0:64], AF.Exp)
        ev_datot = pb.tag("s")
        pb.wait("v", ev_ldt)
        pb.v.tensor_add(nsldt[:], negs[:], ldt_tm[:])
        pb.wait("v", ev_datot)
        pb.v.tensor_add(sd2[:], nsldt[:], pk[4][:, 0:64])
        ev_w2pre = pb.tag("v")
        pb.wait("s", ev_w2pre)
        pb.s.activation(w2b[:], sd2[:], AF.Exp)
        ev_w2b = pb.tag("s")
        if sub < 3:
            pb.barrier(engines="tvsgy")
            continue
        pb.wait("t", e_bfm)
        pb.wait("t", e_cfm)
        pb.wait("t", ev_w2b)
        pb.t.matmul(pk[5][:, 0:128], Bfm[:], Cfm[:], start=True, stop=True)
        ev_g = pb.tag("t")
        pb.wait("v", ev_g)
        pb.v.tensor_copy(GM[:], pk[5][:, 0:128])
        ev_gm = pb.tag("v")
        pb.wait("v", e_xs)
        pb.v.tensor_tensor(xs2[:].rearrange("p (r q) -> p r q", r=64),
                           xs_tm[:].rearrange("p (r q) -> p r q", r=64),
                           bcast3(w2b[:], 64), ALU.mult)
        ev_x2 = pb.tag("v")
        ev_he = None
        ev_ystg_dma = None
        for half in range(2):
            pb.wait("t", ev_x2)
            pb.wait("t", e_btm)
            pb.wait("t", ev_he)
            for q4 in range(4):
                colsl = slice(half * 2048 + q4 * 512, half * 2048 + (q4 + 1) * 512)
                pb.t.matmul(pk[q4][:, :], Btm[:], xs2[:, colsl], start=True, stop=True)
            ev_ha = pb.tag("t")
            pb.wait("v", ev_ha)
            pb.wait("v", ev_ystg_dma)
            for q4 in range(4):
                pb.v.tensor_copy(ystg[:, q4 * 512:(q4 + 1) * 512], pk[q4][:, :])
            ev_he = pb.tag("v")
            pb.wait("y", ev_he)
            ev_ystg_dma = pb.dma("y", out=SCR["hadd"][ch * 128:(ch + 1) * 128, half * 2048:(half + 1) * 2048], in_=ystg[:], key="ystg")
        ev_he_last = ev_he
        if sub < 5:
            pb.barrier(engines="tvsgy")
            continue
        ev_m_prev = None
        ev_sd_prev = None
        ev_p_prev = None
        ev_acc_prev = None
        for half in range(2):
            ev_acc = None
            for qq in range(2):
                quarter = half * 2 + qq
                rsl = slice(quarter * 16, (quarter + 1) * 16)
                pb.wait("g", ev_dta)
                pb.wait("g", ev_m_prev)
                pb.g.tensor_tensor(rq[:].rearrange("p (r q) -> p r q", r=16),
                                   bcast3(dta_tm[:, rsl], 128),
                                   bcast3o(c.U[:], 16), ALU.mult)
                ev_rq = pb.tag("g")
                pb.wait("t", ev_rq)
                pb.wait("t", ev_sd_prev)
                if half == 0 and qq == 0:
                    pb.wait("t", ev_w2pre)
                    pb.wait("t", ev_datot)
                    pb.wait("t", ev_gm)
                    pb.wait("t", ev_he_last)
                for q4 in range(4):
                    colsl = slice(q4 * 512, (q4 + 1) * 512)
                    pb.t.matmul(pk[4 + q4][:, :], c.allone[:], rq[:, colsl], start=True, stop=False)
                    pb.t.matmul(pk[4 + q4][:, :], c.ident[:], c.lneg4[:], start=False, stop=True)
                ev_srep = pb.tag("t")
                pb.wait("v", ev_srep)
                pb.wait("v", ev_p_prev)
                for q4 in range(4):
                    colsl = slice(q4 * 512, (q4 + 1) * 512)
                    pb.v.tensor_tensor(sdq[:, colsl].rearrange("p (r q) -> p r q", r=4),
                                       pk[4 + q4][:, :].rearrange("p (r q) -> p r q", r=4),
                                       bcast3(nsldt[:, quarter * 16 + q4 * 4:quarter * 16 + (q4 + 1) * 4], 128),
                                       ALU.add)
                ev_sd = pb.tag("v")
                ev_sd_prev = ev_sd
                pb.wait("s", ev_sd)
                pb.s.activation(rq[:], sdq[:], AF.Exp)
                ev_p = pb.tag("s")
                ev_p_prev = ev_p
                pb.wait("v", ev_p)
                pb.wait("v", ev_gm)
                pb.wait("v", ev_acc_prev)
                pb.v.tensor_tensor(Mq[:].rearrange("p (r q) -> p r q", r=16),
                                   rq[:].rearrange("p (r q) -> p r q", r=16),
                                   bcast3o(GM[:], 16), ALU.mult)
                ev_m = pb.tag("v")
                ev_m_prev = ev_m
                pb.wait("t", ev_m)
                for hh in range(16):
                    r = quarter * 16 + hh
                    bank = pk[((r * 64) % 2048) // 512]
                    coff = (r * 64) % 512
                    pb.t.matmul(bank[:, coff:coff + 64], Mq[:, hh * 128:(hh + 1) * 128],
                                xs_tm[:, r * 64:(r + 1) * 64],
                                start=((r % 8) == 0), stop=((r % 8) == 7))
                ev_acc = pb.tag("t")
                ev_acc_prev = ev_acc
            pb.wait("v", ev_acc)
            pb.wait("v", ev_ystg_dma)
            for q4 in range(4):
                pb.v.tensor_copy(ystg[:, q4 * 512:(q4 + 1) * 512], pk[q4][:, :])
            ev_ye = pb.tag("v")
            pb.wait("y", ev_ye)
            ev_ystg_dma = pb.dma("y", out=SCR["yi"][ch * 128:(ch + 1) * 128, half * 2048:(half + 1) * 2048], in_=ystg[:], key="ystg")
            pb.wait("t", ev_ye)
        pb.barrier(engines="tvsgy")


def _mamba_chain_e2(pb, c, ar, SCR, pk, PH):
    hld = ar.carve(PH + 29 * 1024, [128, DS], BF16)
    pb.g.memset(c.h[:], 0.0)
    ev = pb.tag("g")
    for ch in range(NCH):
        e_ld = pb.dma("y", out=hld[:], in_=SCR["hadd"][ch * 128:(ch + 1) * 128, :], key="chh")
        pb.wait("v", e_ld)
        pb.wait("v", ev)
        pb.v.tensor_tensor(c.h[:].rearrange("p (r q) -> p r q", r=64),
                           c.h[:].rearrange("p (r q) -> p r q", r=64),
                           bcast3(c.datot_all[:, ch, :], 64), ALU.mult)
        pb.v.tensor_add(c.h[:], c.h[:], hld[:])
        ev = pb.tag("v")
    pb.wait("g", ev)
    ev_snd = pb.dma("g", out=SCR["c2i"], in_=c.h[:])
    pb.wait("g", ev_snd)
    ev_cc = pb.collective("AllGather", ALU.bypass, GROUPS, [SCR["c2i"]], [SCR["c2o"]])
    pb.wait("g", ev_cc)
    ev_rcv = pb.dma("g", out=c.h[:], in_=SCR["c2o"][0:128, :], key="hrecv")
    pb.wait("v", ev_rcv)
    pb.v.tensor_scalar(c.h[:], c.h[:], c.pmask[:, 0:1], None, ALU.mult)
    pb.tag("v")
    pb.barrier(engines="tvg")


def _mamba_p5_scanB(pb, c, ar, W, SCR, pk, PH):
    hld = ar.carve(PH + 29 * 1024, [128, DS], BF16)
    xsld = ar.carve(PH, [128, DS], BF16)
    yild = ar.carve(PH + 8 * 1024, [128, DS], BF16)
    zld = ar.carve(PH + 16 * 1024, [128, DS], BF16)
    Cfm = ar.carve(PH + 24 * 1024 + 768, [128, 128], BF16)
    y2 = ar.carve(PH + 38 * 1024, [128, DS], F32)
    yjunk = ar.carve(PH + 54 * 1024, [128, DS], BF16)
    y4b = ar.carve(PH + 62 * 1024, [128, DS], BF16)
    rss = ar.carve(PH + 70 * 1024, [128, 1], F32)
    rss2 = ar.carve(PH + 70 * 1024 + 32, [128, 1], F32)
    xbsv = SCR["xbcs"]
    xbsc = xbsv.rearrange("(c p) t -> p c t", p=128)
    nwpc = ar.carve(PH + 71 * 1024, [1, 512], F32)
    drsb = ar.carve(PH + 74 * 1024, [1, 64], F32)
    e_dr = pb.dma("y", out=drsb[:], in_=W["Drow"], key="p5dr")
    pb.wait("t", e_dr)
    pb.t.matmul(pk[6][:, 0:64], c.onesrow[:], drsb[0:1, :], start=True, stop=True)
    ev = pb.tag("t")
    pb.wait("v", ev)
    pb.v.tensor_copy(c.Drep[:], pk[6][:, 0:64])
    ev_nw = pb.tag("v")
    for j in range(8):
        e_nw = pb.dma("y", out=nwpc[:], in_=W["nwrow"][0:1, j * 512:(j + 1) * 512], key="p5nw")
        pb.wait("t", e_nw)
        pb.wait("t", ev_nw)
        pb.t.matmul(pk[7][:, :], c.onesrow[:], nwpc[0:1, :], start=True, stop=True)
        evt = pb.tag("t")
        pb.wait("v", evt)
        pb.v.tensor_copy(c.nw_rep[:, j * 512:(j + 1) * 512], pk[7][:, :])
        ev_nw = pb.tag("v")
        pb.wait("y", ev_nw)
    for ch in range(NCH):
        t0 = ch * Q
        e_xs = pb.dmat("y", xsld[:], xbsv[0:DS, t0:t0 + Q], key="p5xs")
        e_yi = pb.dma("y", out=yild[:], in_=SCR["yi"][ch * 128:(ch + 1) * 128, :], key="p5yi")
        e_z = pb.dmat("y", zld[:], SCR["z"][:, t0:t0 + Q], key="p5z")
        e_ha = pb.dma("y", out=hld[:], in_=SCR["hadd"][ch * 128:(ch + 1) * 128, :], key="p5ha")
        e_cf = pb.dma("y", out=Cfm[:], in_=xbsc[:, 33, t0:t0 + Q], key="p5cf")
        pb.wait("t", e_cf)
        if ch == 0:
            pb.wait("t", ev_nw)
        for q8 in range(8):
            pb.t.matmul(pk[q8][:, :], Cfm[:], c.h[:, q8 * 512:(q8 + 1) * 512], start=True, stop=True)
        ev_ys = pb.tag("t")
        pb.wait("v", ev_ys)
        pb.wait("v", e_yi)
        pb.wait("v", e_xs)
        for q8 in range(8):
            colsl = slice(q8 * 512, (q8 + 1) * 512)
            pb.v.tensor_tensor(y2[:, colsl].rearrange("p (r q) -> p r q", r=8),
                               pk[q8][:, :].rearrange("p (r q) -> p r q", r=8),
                               bcast3(c.e0_all[:, ch, q8 * 8:(q8 + 1) * 8], 64), ALU.mult)
        ev_y2a = pb.tag("v")
        pb.v.tensor_tensor(yjunk[:].rearrange("p (r q) -> p r q", r=64),
                           xsld[:].rearrange("p (r q) -> p r q", r=64),
                           bcast3(c.Drep[:], 64), ALU.mult)
        pb.v.tensor_add(y2[:], y2[:], yjunk[:])
        pb.v.tensor_add(y2[:], y2[:], yild[:])
        pb.wait("v", e_z)
        pb.v.tensor_mul(y2[:], y2[:], zld[:])
        ev_y3 = pb.tag("v")
        pb.g.memset(rss[:], 0.0)
        ev_z0 = pb.tag("g")
        pb.wait("s", ev_y3)
        pb.wait("s", ev_z0)
        pb.s.activation(yjunk[:], y2[:], AF.Square, accum_out=rss[:])
        ev_rs = pb.tag("s")
        pb.wait("v", ev_rs)
        pb.v.tensor_scalar(rss[:], rss[:], 1.0 / DS, None, ALU.mult)
        ev_ = pb.tag("v")
        pb.wait("s", ev_)
        pb.s.activation(rss2[:], rss[:], AF.Ln, bias=c.epsc[:, 0:1])
        pb.s.activation(rss2[:], rss2[:], AF.Exp, scale=-0.5)
        ev_ = pb.tag("s")
        pb.wait("v", ev_)
        pb.v.scalar_tensor_tensor(y4b[:], y2[:], rss2[:, 0:1], c.nw_rep[:], ALU.mult, ALU.mult)
        ev_y4 = pb.tag("v")
        pb.wait("y", ev_y4)
        pb.dma("y", out=SCR["y4"][t0:t0 + Q, :], in_=y4b[:])
        pb.wait("v", e_ha)
        pb.v.tensor_tensor(c.h[:].rearrange("p (r q) -> p r q", r=64),
                           c.h[:].rearrange("p (r q) -> p r q", r=64),
                           bcast3(c.datot_all[:, ch, :], 64), ALU.mult)
        pb.v.tensor_add(c.h[:], c.h[:], hld[:])
        pb.tag("v")
        pb.barrier(engines="tvsgy")


def _mamba_p6_outproj(pb, c, ar, W, SCR, pk, x_in, x_out):
    xa = ar.carve(0, [128, NF, T], F32)
    y4f = ar.carve(131072, [128, NF, 512], BF16)
    slab = [ar.carve(163840, [128, NF, 128], BF16), ar.carve(172032, [128, NF, 128], BF16)]
    xv = x_in.rearrange("(c p) t -> p c t", p=128)
    xov = x_out.rearrange("(c p) t -> p c t", p=128)
    wv = W["w_out"].rearrange("(kc kp) o -> kp kc o", kp=128)
    ev_x = {}
    for q in range(4):
        ev_x[q] = pb.dma("y", out=xa[:, q * 8:(q + 1) * 8, :], in_=xv[:, q * 8:(q + 1) * 8, :], key=f"p6x{q}")
    for tt in range(2):
        ev_y4 = {}
        for k in range(NF):
            ev_y4[k] = pb.dmat("y", y4f[:, k, :], SCR["y4"][tt * 512:(tt + 1) * 512, k * 128:(k + 1) * 128], key="p6y4")
        ev_slab = {0: pb.dma("g", out=slab[0][:], in_=wv[:, :, 0:128], key="p6s0"),
                   1: pb.dma("g", out=slab[1][:], in_=wv[:, :, 128:256], key="p6s1")}
        ev_done = {0: None, 1: None}
        for fo in range(NF):
            bi = fo % 2
            obank = pk[0 + bi * 2]
            pb.wait("t", ev_slab[fo])
            pb.wait("t", ev_done[bi])
            if fo == 0:
                pb.wait("t", ev_y4[NF - 1])
            for k in range(NF):
                pb.t.matmul(obank[:, :], slab[bi][:, k, :], y4f[:, k, :], start=(k == 0), stop=(k == NF - 1))
            ev_mm = pb.tag("t")
            if fo + 2 < NF:
                pb.wait("g", ev_mm)
                ev_slab[fo + 2] = pb.dma("g", out=slab[bi][:], in_=wv[:, :, (fo + 2) * 128:(fo + 3) * 128], key=f"p6s{bi}")
            pb.wait("v", ev_mm)
            pb.wait("v", ev_x[fo // 8])
            pb.v.scalar_tensor_tensor(xa[:, fo, tt * 512:(tt + 1) * 512], obank[:, :], 1.0,
                                      xa[:, fo, tt * 512:(tt + 1) * 512], ALU.mult, ALU.add)
            ev_done[bi] = pb.tag("v")
        pb.barrier(engines="tvgy")
    for q in range(4):
        pb.dma("y", out=xov[:, q * 8:(q + 1) * 8, :], in_=xa[:, q * 8:(q + 1) * 8, :])
    pb.barrier(engines="y")


def _prep_mamba_weights(inputs):
    p = {}
    w_in = np.zeros((D, NPC * 128), np.float32)
    w_in[:, :PROJ] = inputs["L0_attn_in_proj_w"]
    p["w_in"] = w_in
    p["w_out"] = np.ascontiguousarray(inputs["L0_attn_out_proj_w"])
    mup = np.zeros(NPC * 128, np.float32)
    mup[:PROJ] = inputs["L0_attn_mup_vector"]
    p["mup"] = np.ascontiguousarray(mup.reshape(NPC, 128).T)
    p["lnw"] = np.ascontiguousarray(inputs["L0_attn_ln_w"].reshape(NF, 128).T)
    p["cw"] = np.ascontiguousarray(inputs["L0_attn_conv_w"].reshape(NCC, 128, 4).transpose(1, 0, 2))
    p["A_log"] = np.ascontiguousarray(inputs["L0_attn_A_log"].reshape(64, 1))
    p["dt_bias"] = np.ascontiguousarray(inputs["L0_attn_dt_bias"].reshape(64, 1))
    p["Drow"] = np.ascontiguousarray(inputs["L0_attn_D"].reshape(1, 64))
    p["nwrow"] = np.ascontiguousarray(inputs["L0_attn_norm_w"].reshape(1, DS))
    p["ones_col"] = np.ones((128, 1), ml_dtypes.bfloat16)
    p["ones_row"] = np.ones((1, 128), np.float32)
    p["epsc"] = np.full((128, 1), 1e-5, np.float32)
    p["Utri"] = np.triu(np.ones((128, 128))).astype(ml_dtypes.bfloat16)
    p["identb"] = np.eye(128).astype(ml_dtypes.bfloat16)
    p["allone"] = np.ones((128, 128), ml_dtypes.bfloat16)
    lneg = (np.tril(np.ones((128, 128)), -1) * -3.0e7).astype(np.float32)
    p["lneg4"] = np.tile(lneg, (1, 4)).astype(ml_dtypes.bfloat16)
    return p


def run_mamba(x_shards, inputs):
    nc = _cached("mamba", build_mamba_launch)
    wp = _prep_mamba_weights(inputs)
    in_maps = []
    for cid in range(NCORES):
        m = dict(wp, x_in=x_shards[cid])
        m["pmask"] = np.full((128, 1), float(cid % 2), np.float32)
        in_maps.append(m)
    res = run_bass_kernel_spmd(nc, in_maps, core_ids=list(range(NCORES)))
    return [res.results[cid]["x_out"] for cid in range(NCORES)]


def build_ffn1_launch():
    nc = bass.Bass("TRN2", target_bir_lowering=False)
    es = contextlib.ExitStack()
    c = Ctx()
    with es:
        x_in = nc.dram_tensor("x_in", [D, T], F32, kind="ExternalInput").ap()
        x_out = nc.dram_tensor("x_out", [D, T], F32, kind="ExternalOutput").ap()
        W = {}
        W["wT"] = nc.dram_tensor("wT", [D, D], F32, kind="ExternalInput").ap()
        W["gwT"] = nc.dram_tensor("gwT", [D, 8], F32, kind="ExternalInput").ap()
        W["ics"] = nc.dram_tensor("ics", [8, D], F32, kind="ExternalInput").ap()
        W["ocs"] = nc.dram_tensor("ocs", [8, D], F32, kind="ExternalInput").ap()
        W["bias"] = nc.dram_tensor("biasw", [128, NF], F32, kind="ExternalInput").ap()
        W["ones_col"] = nc.dram_tensor("ones_col", [128, 1], BF16, kind="ExternalInput").ap()
        W["ones_row"] = nc.dram_tensor("ones_row", [1, 128], F32, kind="ExternalInput").ap()
        W["epsc"] = nc.dram_tensor("epsc", [128, 1], F32, kind="ExternalInput").ap()

        pb = PB(nc, es)
        c.xst = [sb(nc, es, f"xst{i}", [128, T], F32) for i in range(2)]
        c.xb_st = sb(nc, es, "xb_st", [128, 512], BF16)
        c.x2_st = sb(nc, es, "x2_st", [128, 512], BF16)
        c.xs = sb(nc, es, "xs", [128, NF, T], BF16)
        c.slab = [sb(nc, es, f"fslab{i}", [128, NF, 128], BF16) for i in range(2)]
        c.gwt = sb(nc, es, "gwt", [128, NF, 8], BF16)
        c.icst = sb(nc, es, "icst", [8, D], BF16)
        c.ocst = sb(nc, es, "ocst", [8, D], BF16)
        c.biasw = sb(nc, es, "biasw_sb", [128, NF], F32)
        c.routing = sb(nc, es, "routing", [8, T], BF16)
        c.rstd_rep = sb(nc, es, "rstd_rep", [128, T], F32)
        c.mrstd_rep = sb(nc, es, "mrstd_rep", [128, T], F32)
        c.row0 = sb(nc, es, "row0", [1, T], F32)
        c.row1 = sb(nc, es, "row1", [1, T], F32)
        c.f1 = sb(nc, es, "f1", [128, 512], F32)
        c.f2 = sb(nc, es, "f2", [128, 512], F32)
        c.osv = sb(nc, es, "osv", [128, 512], BF16)
        c.csg = sb(nc, es, "csg", [8, 2], F32)
        c.lg = sb(nc, es, "lg", [8, T], F32)
        c.ones_col = sb(nc, es, "onescol", [128, 1], BF16)
        c.ones_row_f = sb(nc, es, "onesrowf", [1, 128], F32)
        c.epsc = sb(nc, es, "epsc_sb", [128, 1], F32)
        pk = [ps(nc, es, f"pb{i}", [128, 512], F32) for i in range(8)]

        with nc.Block() as block:
            pb.dma("y", out=c.ones_col[:], in_=W["ones_col"])
            pb.dma("y", out=c.ones_row_f[:], in_=W["ones_row"])
            pb.dma("y", out=c.epsc[:], in_=W["epsc"])
            pb.dma("g", out=c.gwt[:], in_=W["gwT"].rearrange("(c p) e -> p c e", p=128))
            pb.dma("g", out=c.icst[:], in_=W["ics"])
            pb.dma("g", out=c.ocst[:], in_=W["ocs"])
            pb.dma("g", out=c.biasw[:], in_=W["bias"])
            pb.barrier()
            emit_ffn1(pb, c, W, pk, x_in, x_out)
            pb.finalize(block)
    return nc


def emit_ffn1(pb, c, W, pk, x_in, x_out):
    xv = x_in.rearrange("(c p) t -> p c t", p=128)
    xov = x_out.rearrange("(c p) t -> p c t", p=128)

    ev_ld = {0: pb.dma("y", out=c.xst[0][:], in_=xv[:, 0, :], key="fx0"),
             1: pb.dma("y", out=c.xst[1][:], in_=xv[:, 1, :], key="fx1")}
    ev_mm_last = None
    ev_mm = [None, None]
    for ci in range(NF):
        bi = ci % 2
        for jh in range(2):
            sl = slice(jh * 512, (jh + 1) * 512)
            pb.wait("s", ev_ld[ci])
            pb.wait("s", ev_mm_last)
            pb.s.activation(c.x2_st[:], c.xst[bi][:, sl], AF.Square)
            ev_sq = pb.tag("s")
            pb.wait("v", ev_ld[ci])
            pb.wait("v", ev_mm_last)
            pb.v.tensor_copy(c.xb_st[:], c.xst[bi][:, sl])
            ev_xb = pb.tag("v")
            pb.wait("t", ev_sq)
            pb.wait("t", ev_xb)
            first, last = ci == 0, ci == NF - 1
            pb.t.matmul(pk[0 + jh][0:1, :], c.ones_col[:], c.xb_st[:], start=first, stop=last)
            pb.t.matmul(pk[2 + jh][0:1, :], c.ones_col[:], c.x2_st[:], start=first, stop=last)
            pb.t.matmul(pk[4 + jh][0:8, :], c.gwt[:, ci, :], c.xb_st[:], start=first, stop=last)
            ev_mm[bi] = pb.tag("t")
            ev_mm_last = ev_mm[bi]
        pb.t.matmul(pk[6][0:8, 0:1], c.gwt[:, ci, :], c.ones_col[:], start=(ci == 0), stop=(ci == NF - 1))
        ev_mm[bi] = pb.tag("t")
        ev_mm_last = ev_mm[bi]
        if ci + 2 < NF:
            pb.wait("y", ev_mm[bi])
            ev_ld[ci + 2] = pb.dma("y", out=c.xst[bi][:], in_=xv[:, ci + 2, :], key=f"fx{bi}")
    ev_stats = pb.tag("t")

    r0, r1 = c.row0, c.row1
    pb.wait("v", ev_stats)
    for jh in range(2):
        sl = slice(jh * 512, (jh + 1) * 512)
        pb.v.tensor_scalar(r0[0:1, sl], pk[0 + jh][0:1, :], 1.0 / D, None, ALU.mult)
        pb.v.tensor_scalar(r1[0:1, sl], pk[2 + jh][0:1, :], 1.0 / D, None, ALU.mult)
    pb.v.tensor_tensor(c.f1[0:1, :], r0[0:1, 0:512], r0[0:1, 0:512], ALU.mult)
    pb.v.tensor_sub(r1[0:1, 0:512], r1[0:1, 0:512], c.f1[0:1, :])
    pb.v.tensor_tensor(c.f1[0:1, :], r0[0:1, 512:1024], r0[0:1, 512:1024], ALU.mult)
    pb.v.tensor_sub(r1[0:1, 512:1024], r1[0:1, 512:1024], c.f1[0:1, :])
    ev_var = pb.tag("v")
    pb.wait("s", ev_var)
    pb.s.activation(r1[0:1, :], r1[0:1, :], AF.Ln, bias=c.epsc[0:1, :])
    pb.s.activation(r1[0:1, :], r1[0:1, :], AF.Exp, scale=-0.5)
    ev_srt = pb.tag("s")
    pb.wait("v", ev_srt)
    pb.v.tensor_tensor(r0[0:1, :], r0[0:1, :], r1[0:1, :], ALU.mult)
    ev_rows = pb.tag("v")
    pb.wait("t", ev_rows)
    for jh in range(2):
        sl = slice(jh * 512, (jh + 1) * 512)
        pb.t.matmul(pk[0 + jh][:, :], c.ones_row_f[:], r1[0:1, sl], start=True, stop=True)
        pb.t.matmul(pk[2 + jh][:, :], c.ones_row_f[:], r0[0:1, sl], start=True, stop=True)
    ev_rep = pb.tag("t")
    pb.wait("v", ev_rep)
    pb.v.tensor_copy(c.rstd_rep[:, 0:512], pk[0][:, :])
    pb.v.tensor_copy(c.rstd_rep[:, 512:1024], pk[1][:, :])
    pb.v.tensor_copy(c.mrstd_rep[:, 0:512], pk[2][:, :])
    pb.v.tensor_copy(c.mrstd_rep[:, 512:1024], pk[3][:, :])
    pb.v.tensor_copy(c.csg[0:8, 0:1], pk[6][0:8, 0:1])
    pb.v.tensor_scalar(c.csg[0:8, :], c.csg[0:8, :], -1.0, None, ALU.mult)
    ev_reps = pb.tag("v")

    for jh in range(2):
        sl = slice(jh * 512, (jh + 1) * 512)
        pb.v.tensor_tensor(c.lg[0:8, sl], pk[4 + jh][0:8, :], c.rstd_rep[0:8, sl], ALU.mult)
        pb.v.scalar_tensor_tensor(c.lg[0:8, sl], c.mrstd_rep[0:8, sl], c.csg[0:8, 0:1],
                                  c.lg[0:8, sl], ALU.mult, ALU.add)
    ev_lg = pb.tag("v")
    pb.wait("s", ev_lg)
    pb.s.activation(c.lg[0:8, :], c.lg[0:8, :], AF.Exp)
    ev_exp = pb.tag("s")
    pb.wait("t", ev_exp)
    pb.wait("t", ev_reps)
    pb.wait("v", ev_exp)
    pb.v.tensor_copy(c.routing[0:8, :], c.lg[0:8, :])
    ev_eb = pb.tag("v")
    pb.wait("t", ev_eb)
    for jh in range(2):
        sl = slice(jh * 512, (jh + 1) * 512)
        pb.t.matmul(pk[4 + jh][0:1, :], c.ones_col[0:8, :], c.routing[0:8, sl], start=True, stop=True)
    ev_dn = pb.tag("t")
    pb.wait("s", ev_dn)
    pb.s.activation(c.f1[0:1, 0:512], pk[4][0:1, :], AF.Ln)
    pb.s.activation(c.f1[32:33, 0:512], pk[5][0:1, :], AF.Ln)
    pb.s.activation(c.f1[0:1, 0:512], c.f1[0:1, 0:512], AF.Exp, scale=-1.0)
    pb.s.activation(c.f1[32:33, 0:512], c.f1[32:33, 0:512], AF.Exp, scale=-1.0)
    ev_rc = pb.tag("s")
    pb.wait("v", ev_rc)
    pb.v.tensor_copy(c.row1[0:1, 0:512], c.f1[0:1, 0:512])
    pb.v.tensor_copy(c.row1[0:1, 512:1024], c.f1[32:33, 0:512])
    ev_rcv = pb.tag("v")
    pb.wait("t", ev_rcv)
    pb.t.matmul(pk[6][0:8, :], c.ones_row_f[0:1, 0:8], c.row1[0:1, 0:512], start=True, stop=True)
    pb.t.matmul(pk[7][0:8, :], c.ones_row_f[0:1, 0:8], c.row1[0:1, 512:1024], start=True, stop=True)
    ev_dnr = pb.tag("t")
    pb.wait("v", ev_dnr)
    pb.v.tensor_tensor(c.routing[0:8, 0:512], c.routing[0:8, 0:512], pk[6][0:8, :], ALU.mult)
    pb.v.tensor_tensor(c.routing[0:8, 512:1024], c.routing[0:8, 512:1024], pk[7][0:8, :], ALU.mult)
    ev_rt = pb.tag("v")

    pb.wait("y", ev_rt)
    ev_ld2 = {0: pb.dma("y", out=c.xst[0][:], in_=xv[:, 0, :], key="fx0"),
              1: pb.dma("y", out=c.xst[1][:], in_=xv[:, 1, :], key="fx1")}
    ev_is = {0: None, 1: None}
    ev_xs = [None, None]
    for ci in range(NF):
        bi = ci % 2
        pb.wait("t", ev_is[bi])
        pb.wait("t", ev_rt)
        for jh in range(2):
            sl = slice(jh * 512, (jh + 1) * 512)
            pb.t.matmul(pk[0 + bi * 2 + jh][:, :], c.icst[0:8, ci * 128:(ci + 1) * 128],
                        c.routing[0:8, sl], start=True, stop=True)
        ev_ismm = pb.tag("t")
        pb.wait("v", ev_ld2[ci])
        pb.wait("v", ev_ismm)
        pb.wait("v", ev_xs[bi])
        pb.v.tensor_tensor(c.xst[bi][:], c.xst[bi][:], c.rstd_rep[:], ALU.mult)
        pb.v.tensor_sub(c.xst[bi][:], c.xst[bi][:], c.mrstd_rep[:])
        for jh in range(2):
            sl = slice(jh * 512, (jh + 1) * 512)
            pb.v.tensor_tensor(c.xs[:, ci, sl], c.xst[bi][:, sl], pk[0 + bi * 2 + jh][:, :], ALU.mult)
        ev_xs[bi] = pb.tag("v")
        ev_is[bi] = ev_xs[bi]
        if ci + 2 < NF:
            pb.wait("y", ev_xs[bi])
            ev_ld2[ci + 2] = pb.dma("y", out=c.xst[bi][:], in_=xv[:, ci + 2, :], key=f"fx{bi}")
    pb.barrier(engines="tvsy")

    wv = W["wT"].rearrange("(kc kp) o -> kp kc o", kp=128)
    ev_ws = {0: pb.dma("g", out=c.slab[0][:], in_=wv[:, :, 0:128], key="fs0"),
             1: pb.dma("g", out=c.slab[1][:], in_=wv[:, :, 128:256], key="fs1")}
    ev_sg = {0: None, 1: None}
    ev_x2 = {}
    ev_out = {0: None, 1: None}
    ev_st = {0: None, 1: None}
    for fo in range(NF):
        bi = fo % 2
        pb.wait("s", ev_ws[fo])
        pb.wait("s", ev_sg[bi])
        pb.s.activation(c.slab[bi][:].rearrange("p a b -> p (a b)"),
                        c.slab[bi][:].rearrange("p a b -> p (a b)"), AF.Sign)
        ev_sn = pb.tag("s")
        if fo < 2:
            ev_x2[fo] = pb.dma("y", out=c.xst[bi][:], in_=xv[:, fo, :], key=f"fx{bi}")
        pb.wait("t", ev_sn)
        pb.wait("t", ev_out[bi])
        for jh in range(2):
            sl = slice(jh * 512, (jh + 1) * 512)
            for k in range(NF):
                pb.t.matmul(pk[0 + bi * 2 + jh][:, :], c.slab[bi][:, k, :], c.xs[:, k, sl],
                            start=(k == 0), stop=(k == NF - 1))
        ev_mm = pb.tag("t")
        ev_sg[bi] = ev_mm
        if fo + 2 < NF:
            pb.wait("g", ev_mm)
            ev_ws[fo + 2] = pb.dma("g", out=c.slab[bi][:], in_=wv[:, :, (fo + 2) * 128:(fo + 3) * 128], key=f"fs{bi}")
        pb.wait("t", ev_out[bi])
        for jh in range(2):
            sl = slice(jh * 512, (jh + 1) * 512)
            pb.t.matmul(pk[4 + bi * 2 + jh][0:128, :], c.ocst[0:8, fo * 128:(fo + 1) * 128],
                        c.routing[0:8, sl], start=True, stop=True)
        ev_os = pb.tag("t")
        pb.wait("v", ev_mm)
        pb.wait("v", ev_os)
        pb.wait("v", ev_x2[fo])
        for jh in range(2):
            sl = slice(jh * 512, (jh + 1) * 512)
            pb.v.tensor_copy(c.osv[:, :], pk[4 + bi * 2 + jh][:, :])
            pb.waitf("v", pb.tag("v"))
            pb.v.tensor_tensor(c.f2[:, :], pk[0 + bi * 2 + jh][:, :], c.osv[:, :], ALU.mult)
            pb.waitf("v", pb.tag("v"))
            pb.v.scalar_tensor_tensor(c.f2[:, :], c.f2[:, :], 0.5, c.xst[bi][:, sl], ALU.mult, ALU.add)
            pb.waitf("v", pb.tag("v"))
            pb.v.scalar_tensor_tensor(c.xst[bi][:, sl], c.biasw[:, fo:fo + 1].broadcast_to((128, 512)),
                                      0.5, c.f2[:, :], ALU.mult, ALU.add)
        ev_ep = pb.tag("v")
        pb.wait("y", ev_ep)
        ev_st[bi] = pb.dma("y", out=xov[:, fo, :], in_=c.xst[bi][:], key=f"fxo{bi}")
        ev_out[bi] = ev_ep
        if fo + 2 < NF:
            pb.wait("y", ev_st[bi])
            ev_x2[fo + 2] = pb.dma("y", out=c.xst[bi][:], in_=xv[:, fo + 2, :], key=f"fx{bi}")
    pb.barrier(engines="y")


def _prep_ffn1_weights(inputs):
    p = {}
    p["wT"] = np.ascontiguousarray(inputs["L1_ffn_weight"].T)
    p["gwT"] = np.ascontiguousarray(inputs["L1_ffn_gate_w"].T)
    p["ics"] = np.ascontiguousarray(inputs["L1_ffn_in_channel_scale"])
    p["ocs"] = np.ascontiguousarray(inputs["L1_ffn_out_channel_scale"])
    p["biasw"] = np.ascontiguousarray(inputs["L1_ffn_bias"].reshape(NF, 128).T)
    p["ones_col"] = np.ones((128, 1), ml_dtypes.bfloat16)
    p["ones_row"] = np.ones((1, 128), np.float32)
    p["epsc"] = np.full((128, 1), 1e-5, np.float32)
    return p


def run_ffn1(x_shards, inputs):
    nc = _cached("ffn1", build_ffn1_launch)
    wp = _prep_ffn1_weights(inputs)
    in_maps = [dict(wp, x_in=x_shards[cid]) for cid in range(NCORES)]
    res = run_bass_kernel_spmd(nc, in_maps, core_ids=list(range(NCORES)))
    return [res.results[cid]["x_out"] for cid in range(NCORES)]


NHD = 16
NKV = 4
HDA = 64
LAT = 1024
KVD = 256


def build_attn_launch():
    nc = bass.Bass("TRN2", target_bir_lowering=False)
    es = contextlib.ExitStack()
    c = Ctx()
    with es:
        x_in = nc.dram_tensor("x_in", [D, T], F32, kind="ExternalInput").ap()
        x_out = nc.dram_tensor("x_out", [D, T], F32, kind="ExternalOutput").ap()
        W = {}
        W["wqkvT"] = nc.dram_tensor("wqkvT", [D, 1536], F32, kind="ExternalInput").ap()
        W["woT"] = nc.dram_tensor("woT", [LAT, D], F32, kind="ExternalInput").ap()
        W["qc1T"] = nc.dram_tensor("qc1T", [3, 8, 128, 64], F32, kind="ExternalInput").ap()
        W["qc2T"] = nc.dram_tensor("qc2T", [3, 8, 128, LAT], F32, kind="ExternalInput").ap()
        W["kc1T"] = nc.dram_tensor("kc1T", [3, 2, 128, 64], F32, kind="ExternalInput").ap()
        W["kc2T"] = nc.dram_tensor("kc2T", [3, 4, 128, 128], F32, kind="ExternalInput").ap()
        W["qc1b"] = nc.dram_tensor("qc1b", [128, 8], F32, kind="ExternalInput").ap()
        W["qc2b"] = nc.dram_tensor("qc2b", [128, 8], F32, kind="ExternalInput").ap()
        W["kc1b"] = nc.dram_tensor("kc1b", [128, 2], F32, kind="ExternalInput").ap()
        W["kc2b"] = nc.dram_tensor("kc2b", [128, 2], F32, kind="ExternalInput").ap()
        W["foldq"] = nc.dram_tensor("foldq", [128, 64], BF16, kind="ExternalInput").ap()
        W["foldk"] = nc.dram_tensor("foldk", [128, 64], BF16, kind="ExternalInput").ap()
        W["dupq"] = nc.dram_tensor("dupq", [64, 128], BF16, kind="ExternalInput").ap()
        W["cosq"] = nc.dram_tensor("cosq", [128, 8, 32], F32, kind="ExternalInput").ap()
        W["sinq"] = nc.dram_tensor("sinq", [128, 8, 32], F32, kind="ExternalInput").ap()
        W["Utri"] = nc.dram_tensor("Utri", [128, 128], BF16, kind="ExternalInput").ap()
        W["ktemp"] = nc.dram_tensor("ktemp", [1, 1], F32, kind="ExternalInput").ap()
        W["maskoff"] = nc.dram_tensor("maskoff", [128, 1], F32, kind="ExternalInput").ap()
        W["pmask"] = nc.dram_tensor("pmask", [128, 1], F32, kind="ExternalInput").ap()
        W["ones_col"] = nc.dram_tensor("ones_col", [128, 1], BF16, kind="ExternalInput").ap()
        W["ones_row"] = nc.dram_tensor("ones_row", [1, 128], F32, kind="ExternalInput").ap()
        W["epsc"] = nc.dram_tensor("epsc", [128, 1], F32, kind="ExternalInput").ap()
        cc3_in = nc.dram_tensor("cc3_in", [128, 64], BF16, kind="Internal").ap()
        cc3_out = nc.dram_tensor("cc3_out", [256, 64], BF16, kind="Internal").ap()
        cc4_in = nc.dram_tensor("cc4_in", [128, 6400], BF16, kind="Internal").ap()
        cc4_out = nc.dram_tensor("cc4_out", [256, 6400], BF16, kind="Internal").ap()
        SCR = dict(c3i=cc3_in, c3o=cc3_out, c4i=cc4_in, c4o=cc4_out)

        pb = PB(nc, es)
        ar = Arena(nc, es, "arena", 200704)
        pk = [ps(nc, es, f"pb{i}", [128, 512], F32) for i in range(8)]
        emit_attn(pb, c, ar, W, SCR, pk, x_in, x_out, nc)
    return nc


def emit_attn(pb, c, ar, W, SCR, pk, x_in, x_out, nc):
    c.xln = ar.carve(0, [128, NF, T], BF16)
    c.q2 = ar.carve(0, [128, 8, T], BF16)
    c.k2 = ar.carve(16384, [128, 2, T], BF16)
    c.qdm = ar.carve(20480, [128, 8, T], BF16)
    c.kdm = ar.carve(36864, [128, 4, T], BF16)
    c.rkdm = ar.carve(45056, [128, 4, T], BF16)
    c.vtm = ar.carve(53248, [128, 8, 260], BF16)
    c.rvtm = ar.carve(57600, [128, 8, 260], BF16)
    c.qkv = ar.carve(65536, [128, 12, T + 4], BF16)
    c.atm = ar.carve(90368, [128, 8, T], BF16)
    c.c1q = ar.carve(106752, [128, 8, T + 4], BF16)
    c.afm = ar.carve(106752, [128, 8, T], BF16)
    c.slab = [ar.carve(124160, [128, NF, 128], BF16), ar.carve(132352, [128, NF, 128], BF16)]
    c.qtm = ar.carve(140544, [128, T], BF16)
    c.ktm = ar.carve(142592, [128, 260], BF16)
    c.ktm2 = ar.carve(143168, [128, 256], BF16)
    c.rsq = ar.carve(143744, [128, 16], F32)
    c.rsk = ar.carve(144832, [128, 4], F32)
    c.tmpa = ar.carve(145920, [128, 512], F32)
    c.tmpb = ar.carve(147968, [128, 512], F32)
    c.ptile = ar.carve(150016, [128, T], BF16)
    c.rstd_rep = ar.carve(152064, [128, T], F32)
    c.mrstd_rep = ar.carve(156160, [128, T], F32)
    c.row0 = ar.carve(160256, [1, T], F32)
    c.row1 = ar.carve(164352, [1, T], F32)
    c.xst = [ar.carve(168448, [128, T], F32), ar.carve(172544, [128, T], F32)]
    c.xb_st = ar.carve(176640, [128, 512], BF16)
    c.x2_st = ar.carve(177664, [128, 512], BF16)
    c.kmv = ar.carve(178688, [64, T], BF16)
    c.qmv = ar.carve(180736, [64, T], BF16)
    o_ = 182784
    def _cv(shape, dt):
        nonlocal o_
        esz = 4 if dt == F32 else 2
        n = 1
        for s_ in shape[1:]:
            n *= s_
        r = ar.carve(o_, shape, dt)
        o_ += ((n * esz + 31) // 32) * 32
        return r
    c.qc1b = _cv([128, 8], F32)
    c.qc2b = _cv([128, 8], F32)
    c.kc1b = _cv([128, 2], F32)
    c.kc2b = _cv([128, 2], F32)
    c.foldq = _cv([128, 64], BF16)
    c.foldk = _cv([128, 64], BF16)
    c.dupq = _cv([64, 128], BF16)
    c.cosq = _cv([128, 8, 32], F32)
    c.sinq = _cv([128, 8, 32], F32)
    c.U = _cv([128, 128], BF16)
    c.ktcol = _cv([128, 1], F32)
    c.bias_own = _cv([128, 1], F32)
    c.bias_rcv = _cv([128, 1], F32)
    c.maskoff = _cv([128, 1], F32)
    c.pmask = _cv([128, 1], F32)
    c.onescol = _cv([128, 1], BF16)
    c.onesrow = _cv([1, 128], F32)
    c.epsc = _cv([128, 1], F32)
    c.ktrow = _cv([1, 1], F32)
    c.cslab = [ar.carve(186464, [128, 3, 8, 128], BF16), ar.carve(192608, [128, 3, 8, 128], BF16)]
    c.q1w = ar.carve(186464, [128, 3, 8, 64], BF16)
    c.k1w = ar.carve(192608, [128, 3, 2, 64], BF16)
    c.k2w = ar.carve(192608, [128, 3, 4, 128], BF16)

    xv = x_in.rearrange("(c p) t -> p c t", p=128)
    xov = x_out.rearrange("(c p) t -> p c t", p=128)

    with nc.Block() as block:
        for nm, dst in (("qc1b", c.qc1b), ("qc2b", c.qc2b), ("kc1b", c.kc1b), ("kc2b", c.kc2b),
                        ("foldq", c.foldq), ("foldk", c.foldk), ("dupq", c.dupq),
                        ("cosq", c.cosq), ("sinq", c.sinq), ("Utri", c.U),
                        ("maskoff", c.maskoff), ("pmask", c.pmask), ("ones_col", c.onescol),
                        ("ones_row", c.onesrow), ("epsc", c.epsc), ("ktemp", c.ktrow)):
            pb.dma("y", out=dst[:], in_=W[nm if nm != "ktemp" else "ktemp"], key="cst")
        pb.barrier()

        _attn_ln(pb, c, pk, xv)
        _attn_qkv(pb, c, W, pk)
        _attn_e3(pb, c, SCR)
        pb.barrier()
        _attn_conv(pb, c, W, pk)
        pb.barrier()
        _attn_tokmajor(pb, c, W, pk)
        pb.barrier()
        _attn_e4(pb, c, SCR)
        pb.barrier()
        _attn_core(pb, c, pk)
        pb.barrier()
        _attn_wo(pb, c, W, pk, xv, xov)
        pb.finalize(block)


def _attn_ln(pb, c, pk, xv):
    ev_ld = {0: pb.dma("y", out=c.xst[0][:], in_=xv[:, 0, :], key="ax0"),
             1: pb.dma("y", out=c.xst[1][:], in_=xv[:, 1, :], key="ax1")}
    ev_mm_last = None
    ev_mm = [None, None]
    for ci in range(NF):
        bi = ci % 2
        for jh in range(2):
            sl = slice(jh * 512, (jh + 1) * 512)
            pb.wait("s", ev_ld[ci])
            pb.wait("s", ev_mm_last)
            pb.s.activation(c.x2_st[:], c.xst[bi][:, sl], AF.Square)
            ev_sq = pb.tag("s")
            pb.wait("v", ev_ld[ci])
            pb.wait("v", ev_mm_last)
            pb.v.tensor_copy(c.xb_st[:], c.xst[bi][:, sl])
            ev_xb = pb.tag("v")
            pb.wait("t", ev_sq)
            pb.wait("t", ev_xb)
            first, last = ci == 0, ci == NF - 1
            pb.t.matmul(pk[0 + jh][0:1, :], c.onescol[:], c.xb_st[:], start=first, stop=last)
            pb.t.matmul(pk[2 + jh][0:1, :], c.onescol[:], c.x2_st[:], start=first, stop=last)
            ev_mm[bi] = pb.tag("t")
            ev_mm_last = ev_mm[bi]
        if ci + 2 < NF:
            pb.wait("y", ev_mm[bi])
            ev_ld[ci + 2] = pb.dma("y", out=c.xst[bi][:], in_=xv[:, ci + 2, :], key=f"ax{bi}")
    ev_stats = pb.tag("t")
    r0, r1 = c.row0, c.row1
    pb.wait("v", ev_stats)
    for jh in range(2):
        sl = slice(jh * 512, (jh + 1) * 512)
        pb.v.tensor_scalar(r0[0:1, sl], pk[0 + jh][0:1, :], 1.0 / D, None, ALU.mult)
        pb.v.tensor_scalar(r1[0:1, sl], pk[2 + jh][0:1, :], 1.0 / D, None, ALU.mult)
    pb.v.tensor_tensor(c.tmpa[0:1, :], r0[0:1, 0:512], r0[0:1, 0:512], ALU.mult)
    pb.v.tensor_sub(r1[0:1, 0:512], r1[0:1, 0:512], c.tmpa[0:1, :])
    pb.v.tensor_tensor(c.tmpa[0:1, :], r0[0:1, 512:1024], r0[0:1, 512:1024], ALU.mult)
    pb.v.tensor_sub(r1[0:1, 512:1024], r1[0:1, 512:1024], c.tmpa[0:1, :])
    ev_var = pb.tag("v")
    pb.wait("s", ev_var)
    pb.s.activation(r1[0:1, :], r1[0:1, :], AF.Ln, bias=c.epsc[0:1, :])
    pb.s.activation(r1[0:1, :], r1[0:1, :], AF.Exp, scale=-0.5)
    ev_srt = pb.tag("s")
    pb.wait("v", ev_srt)
    pb.v.tensor_tensor(r0[0:1, :], r0[0:1, :], r1[0:1, :], ALU.mult)
    ev_rows = pb.tag("v")
    pb.wait("t", ev_rows)
    for jh in range(2):
        sl = slice(jh * 512, (jh + 1) * 512)
        pb.t.matmul(pk[0 + jh][:, :], c.onesrow[:], r1[0:1, sl], start=True, stop=True)
        pb.t.matmul(pk[2 + jh][:, :], c.onesrow[:], r0[0:1, sl], start=True, stop=True)
    ev_rep = pb.tag("t")
    pb.wait("v", ev_rep)
    pb.v.tensor_copy(c.rstd_rep[:, 0:512], pk[0][:, :])
    pb.v.tensor_copy(c.rstd_rep[:, 512:1024], pk[1][:, :])
    pb.v.tensor_copy(c.mrstd_rep[:, 0:512], pk[2][:, :])
    pb.v.tensor_copy(c.mrstd_rep[:, 512:1024], pk[3][:, :])
    ev_reps = pb.tag("v")
    pb.wait("y", ev_reps)
    ev_ld2 = {0: pb.dma("y", out=c.xst[0][:], in_=xv[:, 0, :], key="ax0"),
              1: pb.dma("y", out=c.xst[1][:], in_=xv[:, 1, :], key="ax1")}
    ev_ap = [None, None]
    for ci in range(NF):
        bi = ci % 2
        pb.wait("v", ev_ld2[ci])
        pb.v.tensor_tensor(c.xst[bi][:], c.xst[bi][:], c.rstd_rep[:], ALU.mult)
        pb.v.tensor_sub(c.xln[:, ci, :], c.xst[bi][:], c.mrstd_rep[:])
        ev_ap[bi] = pb.tag("v")
        if ci + 2 < NF:
            pb.wait("y", ev_ap[bi])
            ev_ld2[ci + 2] = pb.dma("y", out=c.xst[bi][:], in_=xv[:, ci + 2, :], key=f"ax{bi}")
    pb.barrier(engines="tvsy")
    pb.t.matmul(pk[4][:, 0:1], c.onesrow[:], c.ktrow[0:1, 0:1], start=True, stop=True)
    ev_kt = pb.tag("t")
    pb.wait("s", ev_kt)
    pb.s.activation(c.ktcol[:], pk[4][:, 0:1], AF.Abs)
    ev_ab = pb.tag("s")
    pb.wait("v", ev_ab)
    pb.v.tensor_scalar(c.bias_own[:], c.ktcol[:], -0.15625, None, ALU.mult)
    pb.v.tensor_add(c.bias_rcv[:], c.bias_own[:], c.maskoff[:])
    pb.v.tensor_copy(c.ktcol[:], pk[4][:, 0:1])
    pb.tag("v")
    pb.barrier(engines="tvs")


def _attn_qkv(pb, c, W, pk):
    wv = W["wqkvT"].rearrange("(kc kp) o -> kp kc o", kp=128)
    ev_slab = {0: pb.dma("g", out=c.slab[0][:], in_=wv[:, :, 0:128], key="as0"),
               1: pb.dma("g", out=c.slab[1][:], in_=wv[:, :, 128:256], key="as1")}
    ev_ev = {0: None, 1: None}
    for m in range(12):
        bi = m % 2
        pbk = (pk[0], pk[1]) if bi == 0 else (pk[2], pk[3])
        pb.wait("t", ev_slab[m])
        pb.wait("t", ev_ev[bi])
        for k in range(NF):
            pb.t.matmul(pbk[0][:, :], c.slab[bi][:, k, :], c.xln[:, k, 0:512], start=(k == 0), stop=(k == NF - 1))
            pb.t.matmul(pbk[1][:, :], c.slab[bi][:, k, :], c.xln[:, k, 512:1024], start=(k == 0), stop=(k == NF - 1))
        ev_mm = pb.tag("t")
        if m + 2 < 12:
            pb.wait("g", ev_mm)
            ev_slab[m + 2] = pb.dma("g", out=c.slab[bi][:], in_=wv[:, :, (m + 2) * 128:(m + 3) * 128], key=f"as{bi}")
        pb.wait("v", ev_mm)
        pb.wait("v", ev_ev[bi])
        pb.v.tensor_copy(c.qkv[:, m, 4:516], pbk[0][:, :])
        pb.v.tensor_copy(c.qkv[:, m, 516:1028], pbk[1][:, :])
        ev_ev[bi] = pb.tag("v")
    pb.barrier(engines="tvg")


def _attn_e3(pb, c, SCR):
    pb.v.tensor_copy(c.qtm[:, 0:48].rearrange("p (a b) -> p a b", a=12), c.qkv[:, :, 1024:1028])
    ev_cp = pb.tag("v")
    pb.wait("g", ev_cp)
    ev = pb.dma("g", out=SCR["c3i"], in_=c.qtm[:, 0:64], key="c3s")
    pb.wait("g", ev)
    ev_cc = pb.collective("AllGather", ALU.bypass, GROUPS, [SCR["c3i"]], [SCR["c3o"]], key="cc3")
    pb.wait("g", ev_cc)
    ev2 = pb.dma("g", out=c.qtm[:, 0:64], in_=SCR["c3o"][0:128, :], key="c3r")
    pb.wait("v", ev2)
    pb.v.tensor_scalar(c.qtm[:, 0:64], c.qtm[:, 0:64], c.pmask[:, 0:1], None, ALU.mult)
    pb.v.tensor_copy(c.qkv[:, :, 0:4], c.qtm[:, 0:48].rearrange("p (a b) -> p a b", a=12))
    pb.tag("v")


def _attn_conv(pb, c, W, pk):
    ev_q1w = pb.dma("g", out=c.q1w[:],
                    in_=W["qc1T"].rearrange("k c p o -> p k c o"), key="acs0")
    ev_k1w = pb.dma("g", out=c.k1w[:],
                    in_=W["kc1T"].rearrange("k c p o -> p k c o"), key="acs1")
    NCH1 = [512, 512, 2]
    for qc in range(8):
        bs = 0 if qc % 2 == 0 else 3
        pb.wait("t", ev_q1w)
        for half in range(2):
            po = half * 64
            for tap in range(3):
                off = 0
                for i_, ncl in enumerate(NCH1):
                    pb.t.matmul(pk[bs + i_][po:po + 64, 0:ncl],
                                c.q1w[po:po + 64, tap, qc, 0:64],
                                c.qkv[po:po + 64, qc, tap + off:tap + off + ncl],
                                start=(tap == 0), stop=(tap == 2))
                    off += ncl
        ev_mm = pb.tag("t")
        pb.wait("v", ev_mm)
        off = 0
        for i_, ncl in enumerate(NCH1):
            pb.v.tensor_scalar(c.c1q[:, qc, off:off + ncl], pk[bs + i_][:, 0:ncl],
                               c.qc1b[:, qc:qc + 1], None, ALU.add)
            off += ncl
        ev_ev = pb.tag("v")
        pb.wait("t", ev_ev)
    pb.barrier(engines="tv")
    k1o = c.atm
    k1 = ar_carve_k1(c)
    for kc in range(2):
        bs = 0 if kc % 2 == 0 else 3
        pb.wait("t", ev_k1w)
        for half in range(2):
            po = half * 64
            for tap in range(3):
                off = 0
                for i_, ncl in enumerate(NCH1):
                    pb.t.matmul(pk[bs + i_][po:po + 64, 0:ncl],
                                c.k1w[po:po + 64, tap, kc, 0:64],
                                c.qkv[po:po + 64, 8 + kc, tap + off:tap + off + ncl],
                                start=(tap == 0), stop=(tap == 2))
                    off += ncl
        ev_mm = pb.tag("t")
        pb.wait("v", ev_mm)
        off = 0
        for i_, ncl in enumerate(NCH1):
            pb.v.tensor_scalar(k1[:, kc, off:off + ncl], pk[bs + i_][:, 0:ncl],
                               c.kc1b[:, kc:kc + 1], None, ALU.add)
            off += ncl
        ev_ev = pb.tag("v")
        pb.wait("t", ev_ev)
    pb.barrier(engines="tv")
    for kc in range(2):
        pb.t.matmul(pk[4][0:64, :], c.foldk[:], c.qkv[:, 8 + kc, 4:516], start=(kc == 0), stop=(kc == 1))
        pb.t.matmul(pk[5][0:64, :], c.foldk[:], c.qkv[:, 8 + kc, 516:1028], start=(kc == 0), stop=(kc == 1))
    ev_km = pb.tag("t")
    for qc in range(8):
        pb.t.matmul(pk[6][0:64, :], c.foldq[:], c.qkv[:, qc, 4:516], start=(qc == 0), stop=(qc == 7))
        pb.t.matmul(pk[7][0:64, :], c.foldq[:], c.qkv[:, qc, 516:1028], start=(qc == 0), stop=(qc == 7))
    ev_qm = pb.tag("t")
    pb.wait("v", ev_km)
    pb.v.tensor_copy(c.kmv[:, 0:512], pk[4][0:64, :])
    pb.v.tensor_copy(c.kmv[:, 512:1024], pk[5][0:64, :])
    pb.wait("v", ev_qm)
    pb.v.tensor_copy(c.qmv[:, 0:512], pk[6][0:64, :])
    pb.v.tensor_copy(c.qmv[:, 512:1024], pk[7][0:64, :])
    ev_means = pb.tag("v")
    pb.barrier()
    wv2 = W["qc2T"]
    ev_ws = {0: pb.dma("g", out=c.cslab[0][:], in_=wv2[:, :, :, 0:128].rearrange("k a p o -> p k a o"), key="acs0"),
             1: pb.dma("g", out=c.cslab[1][:], in_=wv2[:, :, :, 128:256].rearrange("k a p o -> p k a o"), key="acs1")}
    ev_done = {0: None, 1: None}
    for oc in range(8):
        bi = oc % 2
        pbk = (pk[0], pk[1]) if bi == 0 else (pk[2], pk[3])
        pb.wait("t", ev_ws[oc])
        pb.wait("t", ev_done[bi])
        if oc == 0:
            pb.wait("t", ev_means)
        for jh in range(2):
            sl0 = jh * 512
            for tap in range(3):
                for ic in range(8):
                    pb.t.matmul(pbk[jh][:, :], c.cslab[bi][:, tap, ic, :],
                                c.c1q[:, ic, tap + sl0:tap + sl0 + 512],
                                start=(tap == 0 and ic == 0), stop=False)
            pb.t.matmul(pbk[jh][:, :], c.dupq[:], c.kmv[:, sl0:sl0 + 512], start=False, stop=True)
        ev_mm = pb.tag("t")
        if oc + 2 < 8:
            pb.wait("g", ev_mm)
            ev_ws[oc + 2] = pb.dma("g", out=c.cslab[bi][:],
                                   in_=wv2[:, :, :, (oc + 2) * 128:(oc + 3) * 128].rearrange("k a p o -> p k a o"),
                                   key=f"acs{bi}")
        pb.wait("v", ev_mm)
        pb.v.tensor_scalar(c.q2[:, oc, 0:512], pbk[0][:, :], c.qc2b[:, oc:oc + 1], None, ALU.add)
        pb.v.tensor_scalar(c.q2[:, oc, 512:1024], pbk[1][:, :], c.qc2b[:, oc:oc + 1], None, ALU.add)
        ev_done[bi] = pb.tag("v")
    pb.barrier(engines="tvg")
    wk2 = W["kc2T"]
    ev_kw = pb.dma("g", out=c.k2w[:],
                   in_=wk2.rearrange("k c p o -> p k c o"), key="acs1")
    for oc in range(2):
        pbk = (pk[4], pk[5]) if oc == 0 else (pk[6], pk[7])
        pb.wait("t", ev_kw)
        for jh in range(2):
            sl0 = jh * 512
            for tap in range(3):
                for ic in range(2):
                    pb.t.matmul(pbk[jh][:, :], c.k2w[:, tap, ic * 2 + oc, :],
                                k1[:, ic, tap + sl0:tap + sl0 + 512],
                                start=(tap == 0 and ic == 0), stop=False)
            pb.t.matmul(pbk[jh][:, :], c.dupq[:], c.qmv[:, sl0:sl0 + 512], start=False, stop=True)
        ev_mm = pb.tag("t")
        pb.wait("v", ev_mm)
        pb.v.tensor_scalar(c.k2[:, oc, 0:512], pbk[0][:, :], c.kc2b[:, oc:oc + 1], None, ALU.add)
        pb.v.tensor_scalar(c.k2[:, oc, 512:1024], pbk[1][:, :], c.kc2b[:, oc:oc + 1], None, ALU.add)
        pb.tag("v")
    pb.barrier(engines="tvg")


def ar_carve_k1(c):
    return c.atm[:].rearrange("p a t -> p (a t)")[:, 0:2 * 1028].rearrange("p (a t) -> p a t", a=2)


def _attn_tokmajor(pb, c, W, pk):
    for tc in range(8):
        t0 = tc * 128
        ev_tq = []
        for fc in range(8):
            ev_tq.append(pb.dmat("y", c.qtm[:, fc * 128:(fc + 1) * 128], c.q2[:, fc, t0:t0 + 128], key="aqtm"))
        ev_tk = []
        for fc in range(2):
            ev_tk.append(pb.dmat("y", c.ktm2[:, fc * 128:(fc + 1) * 128], c.k2[:, fc, t0:t0 + 128], key="aktm"))
        ev_v1 = pb.dmat("y", c.ktm[:, 0:128], c.qkv[:, 10, t0 + 4:t0 + 132], key="avtm")
        ev_v2 = pb.dmat("y", c.ktm[:, 128:256], c.qkv[:, 11, t0 + 3:t0 + 131], key="avtm2")
        for ev in ev_tq:
            pb.wait("v", ev)
        q3 = c.qtm[:].rearrange("p (h d) -> p h d", h=16)
        pb.v.tensor_tensor(c.tmpa[:], c.qtm[:, 0:512], c.qtm[:, 0:512], ALU.mult)
        pb.v.tensor_tensor(c.tmpb[:], c.qtm[:, 512:1024], c.qtm[:, 512:1024], ALU.mult)
        pb.v.tensor_reduce(c.rsq[:, 0:8], c.tmpa[:].rearrange("p (h d) -> p h d", h=8), AX.X, ALU.add)
        pb.v.tensor_reduce(c.rsq[:, 8:16], c.tmpb[:].rearrange("p (h d) -> p h d", h=8), AX.X, ALU.add)
        pb.v.tensor_scalar_max(c.rsq[:], c.rsq[:], 1e-24)
        ev_ss = pb.tag("v")
        pb.wait("s", ev_ss)
        pb.s.activation(c.rsq[:], c.rsq[:], AF.Ln)
        pb.s.activation(c.rsq[:], c.rsq[:], AF.Exp, scale=-0.5)
        ev_rs = pb.tag("s")
        pb.wait("v", ev_rs)
        pb.v.tensor_tensor(q3, q3, bcast3(c.rsq[:], 64), ALU.mult)
        ev_qn = pb.tag("v")
        cos = c.cosq[:, tc, :]
        sin = c.sinq[:, tc, :]
        qeven = c.qtm[:].rearrange("p (h d two) -> p h d two", h=16, d=32)[:, :, :, 0]
        qodd = c.qtm[:].rearrange("p (h d two) -> p h d two", h=16, d=32)[:, :, :, 1]
        cosb = cos.unsqueeze(1).broadcast_to((128, 16, 32))
        sinb = sin.unsqueeze(1).broadcast_to((128, 16, 32))
        t1 = c.tmpa[:].rearrange("p (h d) -> p h d", h=16)
        t2 = c.tmpb[:].rearrange("p (h d) -> p h d", h=16)
        pb.v.tensor_tensor(t1, qeven, cosb, ALU.mult)
        pb.v.tensor_tensor(t2, qodd, sinb, ALU.mult)
        pb.v.tensor_sub(t1, t1, t2)
        t3 = t2
        pb.v.tensor_tensor(t3, qeven, sinb, ALU.mult)
        pb.v.tensor_tensor(qodd, qodd, cosb, ALU.mult)
        pb.v.tensor_add(qodd, qodd, t3)
        pb.v.tensor_copy(qeven, t1)
        ev_qr = pb.tag("v")
        for ev in ev_tk:
            pb.wait("v", ev)
        k3 = c.ktm2[:].rearrange("p (h d) -> p h d", h=4)
        pb.v.tensor_tensor(c.tmpa[:, 0:256], c.ktm2[:], c.ktm2[:], ALU.mult)
        pb.v.tensor_reduce(c.rsk[:], c.tmpa[:, 0:256].rearrange("p (h d) -> p h d", h=4), AX.X, ALU.add)
        pb.v.tensor_scalar_max(c.rsk[:], c.rsk[:], 1e-24)
        ev_ks = pb.tag("v")
        pb.wait("s", ev_ks)
        pb.s.activation(c.rsk[:], c.rsk[:], AF.Ln)
        pb.s.activation(c.rsk[:], c.rsk[:], AF.Exp, scale=-0.5)
        ev_kr = pb.tag("s")
        pb.wait("v", ev_kr)
        pb.v.tensor_scalar(c.rsk[:], c.rsk[:], c.ktcol[:, 0:1], None, ALU.mult)
        pb.v.tensor_tensor(k3, k3, bcast3(c.rsk[:], 64), ALU.mult)
        keven = c.ktm2[:].rearrange("p (h d two) -> p h d two", h=4, d=32)[:, :, :, 0]
        kodd = c.ktm2[:].rearrange("p (h d two) -> p h d two", h=4, d=32)[:, :, :, 1]
        cosb4 = cos.unsqueeze(1).broadcast_to((128, 4, 32))
        sinb4 = sin.unsqueeze(1).broadcast_to((128, 4, 32))
        t1k = c.tmpa[:, 0:128].rearrange("p (h d) -> p h d", h=4)
        t2k = c.tmpb[:, 0:128].rearrange("p (h d) -> p h d", h=4)
        t3k = t2k
        pb.v.tensor_tensor(t1k, keven, cosb4, ALU.mult)
        pb.v.tensor_tensor(t2k, kodd, sinb4, ALU.mult)
        pb.v.tensor_sub(t1k, t1k, t2k)
        pb.v.tensor_tensor(t3k, keven, sinb4, ALU.mult)
        pb.v.tensor_tensor(kodd, kodd, cosb4, ALU.mult)
        pb.v.tensor_add(kodd, kodd, t3k)
        pb.v.tensor_copy(keven, t1k)
        ev_kr2 = pb.tag("v")
        pb.wait("v", ev_v1)
        pb.wait("v", ev_v2)
        pb.v.tensor_copy(c.vtm[:, tc, :].rearrange("p (h e) -> p h e", h=4)[:, :, 0:64],
                         c.ktm[:, 0:256].rearrange("p (h d) -> p h d", h=4))
        ev_vc = pb.tag("v")
        for fc in range(8):
            pb.wait("y", ev_qr)
            pb.dmat("y", c.qdm[:, fc, t0:t0 + 128], c.qtm[:, fc * 128:(fc + 1) * 128], key="aqdm")
        for fc in range(2):
            pb.wait("y", ev_kr2)
            pb.dmat("y", c.kdm[:, fc, t0:t0 + 128], c.ktm2[:, fc * 128:(fc + 1) * 128], key="akdm")
        pb.barrier(engines="vy")
    pb.barrier(engines="tvsgy")
    for kh in range(4):
        pb.g.memset(c.vtm[:, :, kh * 65 + 64:kh * 65 + 65], 1.0)
    ev_on = pb.tag("g")
    for src_c, dst_c in ((0, 2), (1, 3)):
        pb.dma("g", out=c.kdm[0:64, dst_c, :], in_=c.kdm[64:128, src_c, :], key="akswap")
        pb.dma("g", out=c.kdm[64:128, dst_c, :], in_=c.kdm[0:64, src_c, :], key="akswap")
    pb.barrier(engines="tvsgy")


def _attn_e4(pb, c, SCR):
    ev1 = pb.dma("g", out=SCR["c4i"][:, 0:4096], in_=c.kdm[:].rearrange("p a t -> p (a t)"), key="c4s")
    ev2 = pb.dma("g", out=SCR["c4i"][:, 4096:6176], in_=c.vtm[:].rearrange("p a t -> p (a t)"), key="c4s")
    pb.wait("g", ev2)
    ev_cc = pb.collective("AllGather", ALU.bypass, GROUPS, [SCR["c4i"]], [SCR["c4o"]], key="cc4")
    pb.wait("g", ev_cc)
    e3 = pb.dma("g", out=c.rkdm[:].rearrange("p a t -> p (a t)"), in_=SCR["c4o"][0:128, 0:4096], key="c4r")
    e4 = pb.dma("g", out=c.rvtm[:].rearrange("p a t -> p (a t)"), in_=SCR["c4o"][0:128, 4096:6176], key="c4r")
    pb.barrier(engines="tvgy")


def _attn_core(pb, c, pk):
    for h in range(NHD):
        kh = h // 4
        qchunk = h // 2
        qbase = (h % 2) * 64
        kc_nat = kh // 2
        kb_nat = (kh % 2) * 64
        if kb_nat == qbase:
            kch = kc_nat
        else:
            kch = 2 + kc_nat
        ev_pv_prev = None
        ev_exp_prev = None
        for kt in range(16):
            own = kt >= 8
            ktl = kt - 8 if own else kt
            ksrc = (c.kdm if own else c.rkdm)[qbase:qbase + 64, kch, ktl * 128:(ktl + 1) * 128]
            vsrc = (c.vtm if own else c.rvtm)[:, ktl, kh * 65:(kh + 1) * 65]
            q_lo = (ktl * 128) if own else 0
            nq = T - q_lo
            sbank = (kt % 2) * 2
            pb.wait("t", ev_exp_prev if kt >= 2 else None)
            nch = [min(512, nq - i) for i in range(0, nq, 512)]
            off = q_lo
            for i_, ncl in enumerate(nch):
                pb.t.matmul(pk[sbank + i_][:, 0:ncl], ksrc,
                            c.qdm[qbase:qbase + 64, qchunk, off:off + ncl],
                            start=True, stop=True)
                off += ncl
            ev_st = pb.tag("t")
            pb.wait("s", ev_st)
            pb.wait("s", ev_pv_prev)
            off = q_lo
            for i_, ncl in enumerate(nch):
                pb.s.activation(c.ptile[:, off:off + ncl], pk[sbank + i_][:, 0:ncl], AF.Exp,
                                bias=(c.bias_own if own else c.bias_rcv)[:, 0:1], scale=0.125)
                off += ncl
            ev_exp = pb.tag("s")
            ev_exp_prev = ev_exp
            ev_m = ev_exp
            if own:
                pb.wait("v", ev_exp)
                pb.v.tensor_tensor(c.ptile[:, ktl * 128:(ktl + 1) * 128],
                                   c.ptile[:, ktl * 128:(ktl + 1) * 128], c.U[:], ALU.mult)
                ev_m = pb.tag("v")
            pb.wait("t", ev_m)
            qt_lo = ktl if own else 0
            for qt in range(qt_lo, 8):
                bank = pk[4 + qt // 2]
                coff = (qt % 2) * 65
                pb.t.matmul(bank[:, coff:coff + 65], c.ptile[:, qt * 128:(qt + 1) * 128], vsrc,
                            start=(kt == 0 and qt % 2 == 0), stop=(kt == qt + 8))
            ev_pv_prev = pb.tag("t")
        pb.wait("s", ev_pv_prev)
        for qt in range(8):
            bank = pk[4 + qt // 2]
            coff = (qt % 2) * 65
            pb.s.activation(c.rsq[:, 0:1], bank[:, coff + 64:coff + 65], AF.Ln)
            pb.s.activation(c.rsq[:, 0:1], c.rsq[:, 0:1], AF.Exp, scale=-1.0)
            ev_rc = pb.tag("s")
            pb.wait("v", ev_rc)
            pb.v.tensor_scalar(c.atm[:, qt, h * 64:(h + 1) * 64], bank[:, coff:coff + 64],
                               c.rsq[:, 0:1], None, ALU.mult)
            ev_e = pb.tag("v")
            pb.wait("s", ev_e)
        ev_fin = pb.tag("v")
        pb.wait("t", ev_fin)
    pb.barrier(engines="tvs")


def _attn_wo(pb, c, W, pk, xv, xov):
    for qt in range(8):
        for fc in range(8):
            pb.dmat("y", c.afm[:, fc, qt * 128:(qt + 1) * 128], c.atm[:, qt, fc * 128:(fc + 1) * 128], key="aafm")
    ev_t = Ev("K_aafm", pb.counts["K_aafm"])
    wv = W["woT"].rearrange("(kc kp) o -> kp kc o", kp=128)
    ev_ws = {0: pb.dma("g", out=c.slab[0][:, 0:8, :], in_=wv[:, :, 0:128], key="as0"),
             1: pb.dma("g", out=c.slab[1][:, 0:8, :], in_=wv[:, :, 128:256], key="as1")}
    ev_x = {0: pb.dma("y", out=c.xst[0][:], in_=xv[:, 0, :], key="ax0"),
            1: pb.dma("y", out=c.xst[1][:], in_=xv[:, 1, :], key="ax1")}
    ev_done = {0: None, 1: None}
    ev_st = {0: None, 1: None}
    for fo in range(NF):
        bi = fo % 2
        pbk = (pk[0], pk[1]) if bi == 0 else (pk[2], pk[3])
        pb.wait("t", ev_ws[fo])
        pb.wait("t", ev_t)
        pb.wait("t", ev_done[bi])
        for k in range(8):
            pb.t.matmul(pbk[0][:, :], c.slab[bi][:, k, :], c.afm[:, k, 0:512], start=(k == 0), stop=(k == 7))
            pb.t.matmul(pbk[1][:, :], c.slab[bi][:, k, :], c.afm[:, k, 512:1024], start=(k == 0), stop=(k == 7))
        ev_mm = pb.tag("t")
        if fo + 2 < NF:
            pb.wait("g", ev_mm)
            ev_ws[fo + 2] = pb.dma("g", out=c.slab[bi][:, 0:8, :], in_=wv[:, :, (fo + 2) * 128:(fo + 3) * 128], key=f"as{bi}")
        pb.wait("v", ev_mm)
        pb.wait("v", ev_x[fo])
        pb.v.scalar_tensor_tensor(c.xst[bi][:, 0:512], pbk[0][:, :], 1.0, c.xst[bi][:, 0:512], ALU.mult, ALU.add)
        pb.v.scalar_tensor_tensor(c.xst[bi][:, 512:1024], pbk[1][:, :], 1.0, c.xst[bi][:, 512:1024], ALU.mult, ALU.add)
        ev_ep = pb.tag("v")
        ev_done[bi] = ev_ep
        pb.wait("y", ev_ep)
        ev_st[bi] = pb.dma("y", out=xov[:, fo, :], in_=c.xst[bi][:], key=f"axo{bi}")
        if fo + 2 < NF:
            pb.wait("y", ev_st[bi])
            ev_x[fo + 2] = pb.dma("y", out=c.xst[bi][:], in_=xv[:, fo + 2, :], key=f"ax{bi}")
    pb.barrier(engines="y")


def _prep_attn_weights(inputs):
    p = {}
    p["wqkvT"] = np.ascontiguousarray(inputs["L1_attn_w_qkv"].T)
    p["woT"] = np.ascontiguousarray(inputs["L1_attn_w_o"].T)
    qc1 = inputs["L1_attn_qc1_w"]
    qc1T = np.zeros((3, 8, 128, 64), np.float32)
    for g in range(16):
        for k in range(3):
            qc1T[k, g // 2, (g % 2) * 64:(g % 2) * 64 + 64, :] = qc1[g * 64:(g + 1) * 64, :, k].T
    p["qc1T"] = qc1T
    qc2 = inputs["L1_attn_qc2_w"]
    qc2T = np.zeros((3, 8, 128, LAT), np.float32)
    for k in range(3):
        qc2T[k] = qc2[:, :, k].T.reshape(8, 128, LAT)
    p["qc2T"] = qc2T
    kc1 = inputs["L1_attn_kc1_w"]
    kc1T = np.zeros((3, 2, 128, 64), np.float32)
    for g in range(4):
        for k in range(3):
            kc1T[k, g // 2, (g % 2) * 64:(g % 2) * 64 + 64, :] = kc1[g * 64:(g + 1) * 64, :, k].T
    p["kc1T"] = kc1T
    kc2 = inputs["L1_attn_kc2_w"]
    kc2T = np.zeros((3, 4, 128, 128), np.float32)
    for k in range(3):
        t_ = kc2[:, :, k].T.reshape(2, 128, 2, 128)
        kc2T[k] = t_.transpose(0, 2, 1, 3).reshape(4, 128, 128)
    p["kc2T"] = kc2T
    p["qc1b"] = np.ascontiguousarray(inputs["L1_attn_qc1_b"].reshape(8, 128).T)
    p["qc2b"] = np.ascontiguousarray(inputs["L1_attn_qc2_b"].reshape(8, 128).T)
    p["kc1b"] = np.ascontiguousarray(inputs["L1_attn_kc1_b"].reshape(2, 128).T)
    p["kc2b"] = np.ascontiguousarray(inputs["L1_attn_kc2_b"].reshape(2, 128).T)
    foldq = np.zeros((128, 64), np.float32)
    for pp in range(128):
        foldq[pp, pp % 64] = 1.0 / 16
    p["foldq"] = foldq.astype(ml_dtypes.bfloat16)
    foldk = (foldq * 4).astype(ml_dtypes.bfloat16)
    p["foldk"] = foldk
    dupq = np.zeros((64, 128), np.float32)
    for o in range(128):
        dupq[o % 64, o] = 0.5
    p["dupq"] = dupq.astype(ml_dtypes.bfloat16)
    p["Utri"] = np.triu(np.ones((128, 128))).astype(ml_dtypes.bfloat16)
    p["ones_col"] = np.ones((128, 1), ml_dtypes.bfloat16)
    p["ones_row"] = np.ones((1, 128), np.float32)
    p["epsc"] = np.full((128, 1), 1e-5, np.float32)
    p["ktemp"] = np.ascontiguousarray(inputs["L1_attn_key_temp"].reshape(1, 1))
    return p


def run_attn(x_shards, inputs):
    nc = _cached("attn", build_attn_launch)
    wp = _prep_attn_weights(inputs)
    in_maps = []
    inv = 1.0 / 10000.0 ** (np.arange(0, HDA, 2, dtype=np.float64) / HDA)
    for cid in range(NCORES):
        h = cid % 2
        pos = np.arange(h * T, (h + 1) * T, dtype=np.float64)
        ang = pos[:, None] * inv[None, :]
        m = dict(wp, x_in=x_shards[cid])
        m["cosq"] = np.cos(ang).astype(np.float32).reshape(8, 128, 32).transpose(1, 0, 2).copy()
        m["sinq"] = np.sin(ang).astype(np.float32).reshape(8, 128, 32).transpose(1, 0, 2).copy()
        m["pmask"] = np.full((128, 1), float(h), np.float32)
        m["maskoff"] = np.full((128, 1), 0.0 if h == 1 else -30000.0, np.float32)
        in_maps.append(m)
    res = run_bass_kernel_spmd(nc, in_maps, core_ids=list(range(NCORES)))
    return [res.results[cid]["x_out"] for cid in range(NCORES)]
